# version 03 (final)
# baseline (speedup 1.0000x reference)
from contextlib import ExitStack

import numpy as np
import ml_dtypes

import concourse.bass as bass
import concourse.mybir as mybir
from concourse.bass_utils import run_bass_kernel_spmd

F32 = mybir.dt.float32
BF16 = mybir.dt.bfloat16
AF = mybir.ActivationFunctionType
ALU = mybir.AluOpType
AX = mybir.AxisListType

DETECT_RACES = False

T = 2048
D = 1024
NT = 16
KC = 8
BIGNEG = -1e30



class Sem:
    def __init__(self, prog, name):
        self.h = prog.stack.enter_context(prog.nc.semaphore(name))
        self.n = 0

    def next(self, k=1):
        self.n += k
        return self.n


class Prog:

    def __init__(self, name):
        self.name = name
        self.nc = bass.Bass("TRN2", target_bir_lowering=False,
                            detect_race_conditions=DETECT_RACES)
        self.stack = ExitStack()
        self.items = []
        self.sems = {}
        self.consts = {}

    def sem(self, name):
        if name not in self.sems:
            self.sems[name] = Sem(self, name)
        return self.sems[name]

    def dram_in(self, name, shape, dtype=F32):
        return self.nc.dram_tensor(name, shape, dtype, kind="ExternalInput").ap()

    def dram_out(self, name, shape, dtype=F32):
        return self.nc.dram_tensor(name, shape, dtype, kind="ExternalOutput").ap()

    def const_in(self, name, arr):
        self.consts[name] = np.ascontiguousarray(arr)
        dt = F32 if arr.dtype == np.float32 else BF16
        return self.nc.dram_tensor(name, list(arr.shape), dt,
                                   kind="ExternalInput").ap()

    def sbuf(self, name, shape, dtype):
        return self.stack.enter_context(self.nc.sbuf_tensor(name, shape, dtype))

    def psum(self, name, shape, dtype=F32):
        return self.stack.enter_context(self.nc.psum_tensor(name, shape, dtype))

    def on(self, eng, fn):
        self.items.append((eng, fn))

    def wait(self, eng, sem, val):
        self.on(eng, lambda e, s=sem.h, v=val: e.wait_ge(s, v))

    def finalize(self):
        nc = self.nc
        items = self.items
        with nc.Block() as block:
            @block.sync
            def _(e):
                for en, fn in items:
                    if en == "sync":
                        fn(e)

            @block.scalar
            def _(e):
                for en, fn in items:
                    if en == "scalar":
                        fn(e)

            @block.vector
            def _(e):
                for en, fn in items:
                    if en == "vector":
                        fn(e)

            @block.tensor
            def _(e):
                for en, fn in items:
                    if en == "tensor":
                        fn(e)

            @block.gpsimd
            def _(e):
                for en, fn in items:
                    if en == "gpsimd":
                        fn(e)
        self.stack.close()
        return nc


def ap(t, offset, dims):
    base = t.ap()
    return bass.AP(base.tensor, base.offset + offset, [base.ap[0]] + dims)


def host_consts_common():
    c = {}
    c["ident"] = np.eye(128, dtype=ml_dtypes.bfloat16)
    m = np.zeros((128, 128), np.float32)
    m[np.triu_indices(128, 1)] = BIGNEG
    c["mask_causal"] = m
    ml_ = np.zeros((128, 128), np.float32)
    ml_[np.tril_indices(128, -1)] = BIGNEG
    c["mask_lead"] = ml_
    return c



def emit_ln_to_fm(P, src_fn, tiles, xn_fm, ps_trA, ps_trB, chunk_stride,
                  data_col0, pref, eps=1e-5):
    nti = len(tiles)
    stats_s = P.sbuf(pref + "_s", [128, nti], F32)
    stats_q = P.sbuf(pref + "_q", [128, nti], F32)
    rstd = P.sbuf(pref + "_rstd", [128, nti], F32)
    negmr = P.sbuf(pref + "_negmr", [128, nti], F32)
    scratch = P.sbuf(pref + "_scr", [128, nti], F32)
    junk, xn_tm = P.ln_scratch

    s_stat = P.sem(pref + "_stat")
    s_prep = P.sem(pref + "_prep")
    s_sqrt = P.sem(pref + "_sqrt")
    s_norm = P.sem(pref + "_norm")
    s_tr = P.sem(pref + "_tr")
    s_fm = P.sem(pref + "_fm")
    ident = P.ident_sb

    s_sq = P.sem(pref + "_sq")
    for j, i in enumerate(tiles):
        xt = src_fn(i)
        s_stat.next(1)
        P.wait("scalar", s_sq, j)
        s_sq.next(1)
        P.on("scalar", lambda e, xt=xt, j=j, s=s_sq.h: e.activation(
            junk[:, :], xt, AF.Square,
            accum_out=stats_q[:, j:j + 1]).then_inc(s, 1))
        P.on("vector", lambda e, xt=xt, j=j, s=s_stat.h: e.reduce_sum(
            stats_s[:, j:j + 1], xt, axis=AX.X).then_inc(s, 1))

    P.wait("vector", s_stat, s_stat.n)
    P.wait("vector", s_sq, s_sq.n)
    v0 = s_prep.n
    s_prep.next(4)
    P.on("vector", lambda e, s=s_prep.h: e.tensor_scalar(
        stats_s[:, :], stats_s[:, :], 1.0 / D, None,
        ALU.mult).then_inc(s, 1))
    P.wait("vector", s_prep, v0 + 1)
    P.on("vector", lambda e, s=s_prep.h: e.tensor_scalar(
        stats_q[:, :], stats_q[:, :], 1.0 / D, float(eps),
        ALU.mult, ALU.add).then_inc(s, 1))
    P.wait("vector", s_prep, v0 + 2)
    P.on("vector", lambda e, s=s_prep.h: e.scalar_tensor_tensor(
        scratch[:, :], stats_s[:, :], 1.0, stats_s[:, :],
        ALU.mult, ALU.mult).then_inc(s, 1))
    P.wait("vector", s_prep, v0 + 3)
    P.on("vector", lambda e, s=s_prep.h: e.tensor_sub(
        stats_q[:, :], stats_q[:, :], scratch[:, :]).then_inc(s, 1))

    P.wait("scalar", s_prep, v0 + 4)
    v2 = s_sqrt.next(1)
    P.on("scalar", lambda e, s=s_sqrt.h: e.activation(
        rstd[:, :], stats_q[:, :], AF.Sqrt).then_inc(s, 1))
    P.wait("vector", s_sqrt, v2)
    v4 = s_prep.n
    s_prep.next(2)
    P.on("vector", lambda e, s=s_prep.h: e.reciprocal(
        rstd[:, :], rstd[:, :]).then_inc(s, 1))
    P.wait("vector", s_prep, v4 + 1)
    P.on("vector", lambda e, s=s_prep.h: e.scalar_tensor_tensor(
        negmr[:, :], stats_s[:, :], -1.0, rstd[:, :],
        ALU.mult, ALU.mult).then_inc(s, 1))
    P.wait("scalar", s_prep, v4 + 2)

    for j, i in enumerate(tiles):
        buf = xn_tm[j % 2]
        psum = ps_trA if j % 2 == 0 else ps_trB
        if j >= 2:
            P.wait("scalar", s_tr, j - 1)
        xt = src_fn(i)
        vn = s_norm.next(1)
        P.on("scalar", lambda e, xt=xt, buf=buf, j=j, s=s_norm.h: e.activation(
            buf[:, :], xt, AF.Identity, bias=negmr[:, j:j + 1],
            scale=rstd[:, j:j + 1]).then_inc(s, 1))
        P.wait("tensor", s_norm, vn)
        if j >= 2:
            P.wait("tensor", s_fm, j - 1)
        vt = s_tr.next(1)

        def _tr(e, buf=buf, psum=psum, s=s_tr.h):
            for c in range(KC):
                ins = e.transpose(psum[:, c * 128:(c + 1) * 128],
                                  buf[:, c * 128:(c + 1) * 128], ident[:, :])
            ins.then_inc(s, 1)
        P.on("tensor", _tr)
        P.wait("vector", s_tr, vt)
        s_fm.next(1)
        out_ap = ap(xn_fm, data_col0 + i * 128,
                    [[chunk_stride, KC], [1, 128]])
        P.on("vector", lambda e, psum=psum, o=out_ap, s=s_fm.h: e.tensor_copy(
            o, psum[:, :]).then_inc(s, 1))
    return s_fm, s_norm



def build_l0_attn(stage="full"):
    P = Prog("l0attn")
    x_in = P.dram_in("x", [T, D])
    wq_d = P.dram_in("Wq", [D, D])
    wk_d = P.dram_in("Wk", [D, D])
    wv_d = P.dram_in("Wv", [D, D])
    wo_d = P.dram_in("Wo", [D, D])
    x_out = P.dram_out("xo", [T, D])

    cc = host_consts_common()
    ident_d = P.const_in("c_ident", cc["ident"])
    mask_d = P.const_in("c_mask", cc["mask_causal"])

    P.ident_sb = P.sbuf("ident", [128, 128], BF16)
    mask_sb = P.sbuf("mask", [128, 128], F32)
    P.ln_scratch = (P.sbuf("lnjunk", [128, D], BF16),
                    [P.sbuf(f"lntm{j}", [128, D], BF16) for j in range(2)])
    x_half = P.sbuf("x_half", [128, 8 * D], F32)
    q_fm = P.sbuf("q_fm", [128, KC * T], BF16)
    k_fm = P.sbuf("k_fm", [128, KC * T], BF16)
    v_tm = P.sbuf("v_tm", [128, NT * D], BF16)
    w_sb = [P.sbuf("w0", [128, KC * D], BF16)]
    xn_fm = P.sbuf("xn_fm", [128, KC * T], BF16)
    p_tm = P.sbuf("p_tm", [128, T], BF16)
    pt_sb = P.sbuf("pt_sb", [128, T], BF16)
    av_tm = P.sbuf("av_tm", [128, D], BF16)
    av_fm = P.sbuf("av_fm", [128, D], BF16)
    nmax = P.sbuf("nmax", [128, 1], F32)
    rsum = P.sbuf("rsum", [128, 1], F32)
    rinv = P.sbuf("rinv", [128, 1], F32)
    xadd = [P.sbuf(f"xadd{j}", [128, D], F32) for j in range(2)]

    ps_big = P.psum("ps_big", [128, 2048], F32)
    ps_trA = P.psum("ps_trA", [128, 1024], BF16)
    ps_trB = P.psum("ps_trB", [128, 1024], BF16)
    ps_av = P.psum("ps_av", [128, 1024], F32)

    s_x = P.sem("s_x")
    s_c = P.sem("s_c")
    s_w = P.sem("w")
    s_st = [P.sem("st0"), P.sem("st1")]

    x_t = [x_in[h * 1024:(h + 1) * 1024, :].rearrange(
        "(n p) d -> p n d", p=128) for h in range(2)]
    vx0 = s_x.next(16)
    P.on("sync", lambda e, s=s_x.h: e.dma_start(
        out=x_half[:, :], in_=x_t[0]).then_inc(s, 16))
    vc = s_c.next(32)
    P.on("sync", lambda e, s=s_c.h: e.dma_start(
        out=P.ident_sb[:, :], in_=ident_d[:, :]).then_inc(s, 16))
    P.on("sync", lambda e, s=s_c.h: e.dma_start(
        out=mask_sb[:, :], in_=mask_d[:, :]).then_inc(s, 16))

    P.ph_marks = []
    s_mm_g = P.sem("mm")
    wq_v = s_w.next(16)
    P.on("gpsimd", lambda e, s=s_w.h: e.dma_start(
        out=w_sb[0][:, :], in_=wq_d.rearrange("(c p) n -> p c n", p=128)
    ).then_inc(s, 16))
    P.on("gpsimd", lambda e, s=s_mm_g.h: e.wait_ge(s, 32))
    wk_v = s_w.next(16)
    P.on("gpsimd", lambda e, s=s_w.h: e.dma_start(
        out=w_sb[0][:, :], in_=wk_d.rearrange("(c p) n -> p c n", p=128)
    ).then_inc(s, 16))
    P.on("gpsimd", lambda e, s=s_mm_g.h: e.wait_ge(s, 64))
    wv_v = s_w.next(16)
    P.on("gpsimd", lambda e, s=s_w.h: e.dma_start(
        out=w_sb[0][:, :], in_=wv_d.rearrange("(c p) n -> p c n", p=128)
    ).then_inc(s, 16))
    P.on("gpsimd", lambda e, s=s_mm_g.h: e.wait_ge(s, 96))
    wo_v = s_w.next(16)
    P.on("gpsimd", lambda e, s=s_w.h: e.dma_start(
        out=w_sb[0][:, :], in_=wo_d.rearrange("(c p) n -> p c n", p=128)
    ).then_inc(s, 16))

    P.wait("scalar", s_x, vx0)
    P.wait("vector", s_x, vx0)
    P.wait("vector", s_c, vc)
    P.wait("tensor", s_c, vc)

    def srcA(i):
        return ap(x_half, i * D, [[1, D]])
    s_fmA, s_normA = emit_ln_to_fm(P, srcA, list(range(8)), xn_fm,
                                   ps_trA, ps_trB, T, 0, "lnA")
    P.wait("sync", s_normA, 8)
    vx1 = s_x.next(16)
    P.on("sync", lambda e, s=s_x.h: e.dma_start(
        out=x_half[:, :], in_=x_t[1]).then_inc(s, 16))
    P.wait("scalar", s_x, vx1)
    P.wait("vector", s_x, vx1)
    P.wait("tensor", s_fmA, 8)

    def srcB(i):
        return ap(x_half, (i - 8) * D, [[1, D]])
    s_fmB, s_normB = emit_ln_to_fm(P, srcB, list(range(8, 16)), xn_fm,
                                   ps_trA, ps_trB, T, 0, "lnB")

    if stage == "ln":
        dbg = P.dram_out("dbg", [128, KC * T], BF16)
        s_dbg = P.sem("dbg")
        P.wait("sync", s_fmA, 8)
        P.wait("sync", s_fmB, 8)
        P.on("sync", lambda e, s=s_dbg.h: e.dma_start(
            out=dbg[:, :], in_=xn_fm[:, :]).then_inc(s, 16))
        P.wait("sync", s_dbg, 16)
        return P

    s_mm = P.sem("mm")
    s_ev = P.sem("ev")
    P.wait("tensor", s_fmB, 8)

    def proj_fm(dst, w_buf, w_ready, scale):
        P.wait("tensor", s_w, w_ready)
        for oc in range(KC):
            for tb in range(4):
                reg = tb % 4
                fillv = s_mm.next(1)
                nfill = oc * 4 + tb
                if nfill >= 4:
                    P.wait("tensor", s_ev, s_ev.n - 3)

                def _mm(e, oc=oc, tb=tb, reg=reg, w_buf=w_buf, s=s_mm.h):
                    for kc in range(KC):
                        ins = e.matmul(ps_big[:, reg * 512:(reg + 1) * 512],
                                       w_buf[:, kc * D + oc * 128:
                                             kc * D + oc * 128 + 128],
                                       ap(xn_fm, kc * T + tb * 512,
                                          [[1, 512]]),
                                       start=(kc == 0), stop=(kc == KC - 1))
                    ins.then_inc(s, 1)
                P.on("tensor", _mm)
                P.wait("scalar", s_mm, fillv)
                s_ev.next(1)
                P.on("scalar", lambda e, oc=oc, tb=tb, reg=reg, dst=dst,
                     scale=scale, s=s_ev.h: e.activation(
                         ap(dst, oc * T + tb * 512, [[1, 512]]),
                         ps_big[:, reg * 512:(reg + 1) * 512],
                         AF.Copy, scale=scale).then_inc(s, 1))

    def proj_tm(dst, w_buf, w_ready):
        P.wait("tensor", s_w, w_ready)
        for i in range(NT):
            for nc2 in range(2):
                reg = (i * 2 + nc2) % 4
                fillv = s_mm.next(1)
                P.wait("tensor", s_ev, s_ev.n - 3)

                def _mm(e, i=i, nc2=nc2, reg=reg, w_buf=w_buf, s=s_mm.h):
                    for kc in range(KC):
                        ins = e.matmul(ps_big[:, reg * 512:(reg + 1) * 512],
                                       ap(xn_fm, kc * T + i * 128, [[1, 128]]),
                                       w_buf[:, kc * D + nc2 * 512:
                                             kc * D + nc2 * 512 + 512],
                                       start=(kc == 0), stop=(kc == KC - 1))
                    ins.then_inc(s, 1)
                P.on("tensor", _mm)
                P.wait("scalar", s_mm, fillv)
                s_ev.next(1)
                P.on("scalar", lambda e, i=i, nc2=nc2, reg=reg, dst=dst,
                     s=s_ev.h: e.activation(
                         ap(dst, i * D + nc2 * 512, [[1, 512]]),
                         ps_big[:, reg * 512:(reg + 1) * 512],
                         AF.Copy).then_inc(s, 1))

    proj_fm(q_fm, w_sb[0], wq_v, 1.0 / 32.0)
    P.ph_marks.append(s_mm.n)
    proj_fm(k_fm, w_sb[0], wk_v, 1.0)
    P.ph_marks.append(s_mm.n)
    proj_tm(v_tm, w_sb[0], wv_v)
    P.ph_marks.append(s_mm.n)
    if stage == "proj":
        dbg = P.dram_out("dbg", [128, KC * T], BF16)
        dbg2 = P.dram_out("dbg2", [128, NT * D], BF16)
        s_dbg = P.sem("dbg")
        P.wait("sync", s_ev, s_ev.n)
        P.on("sync", lambda e, s=s_dbg.h: e.dma_start(
            out=dbg[:, :], in_=q_fm[:, :]).then_inc(s, 16))
        P.on("sync", lambda e, s=s_dbg.h: e.dma_start(
            out=dbg2[:, :], in_=v_tm[:, :]).then_inc(s, 16))
        P.wait("sync", s_dbg, 32)
        return P

    P.wait("tensor", s_ev, s_ev.n)
    P.wait("vector", s_ev, s_ev.n)
    P.wait("tensor", s_w, wo_v)

    sA = P.sem("sA")
    sM = P.sem("sM")
    sE = P.sem("sE")
    sPT = P.sem("sPT")
    sPE = P.sem("sPE")
    sAV = P.sem("sAV")
    sAVE = P.sem("sAVE")
    sAT = P.sem("sAT")
    sATE = P.sem("sATE")
    sWO = P.sem("sWO")
    sADD = P.sem("sADD")
    sR = P.sem("sR")
    s_xa = [P.sem("s_xa0"), P.sem("s_xa1")]
    cum_groups = 0

    for qi in range(NT):
        L = (qi + 1) * 128
        nsb = (L + 511) // 512
        P.wait("tensor", sE, qi)

        def _sc(e, qi=qi, L=L, nsb=nsb, s=sA.h):
            for sb in range(nsb):
                w = min(512, L - sb * 512)
                for kc in range(KC):
                    ins = e.matmul(ps_big[:, sb * 512:sb * 512 + w],
                                   ap(q_fm, kc * T + qi * 128, [[1, 128]]),
                                   ap(k_fm, kc * T + sb * 512, [[1, w]]),
                                   start=(kc == 0), stop=(kc == KC - 1))
            ins.then_inc(s, 1)
        P.on("tensor", _sc)
        vA = sA.next(1)
        P.wait("vector", sA, vA)
        vM = sM.next(1)
        P.on("vector", lambda e, qi=qi, s=sM.h: e.tensor_tensor(
            ps_big[:, qi * 128:qi * 128 + 128],
            ps_big[:, qi * 128:qi * 128 + 128], mask_sb[:, :],
            ALU.add).then_inc(s, 1))
        P.wait("vector", sM, vM)
        vM2 = sM.next(1)
        P.on("vector", lambda e, L=L, s=sM.h: e.reduce_max(
            nmax[:, :], ps_big[:, 0:L], axis=AX.X,
            negate=True).then_inc(s, 1))
        P.wait("scalar", sM, vM2)
        P.wait("scalar", sPT, cum_groups)
        vE = sE.next(1)
        P.on("scalar", lambda e, L=L, s=sE.h: e.activation(
            p_tm[:, 0:L], ps_big[:, 0:L], AF.Exp, bias=nmax[:, 0:1],
            accum_out=rsum[:, 0:1]).then_inc(s, 1))
        P.wait("vector", sE, vE)
        vR = sR.next(1)
        P.on("vector", lambda e, s=sR.h: e.reciprocal(
            rinv[:, :], rsum[:, :]).then_inc(s, 1))

        ntile = qi + 1
        ngr = (ntile + 7) // 8
        P.wait("tensor", sE, vE)
        for g in range(ngr):
            n_in_g = min(8, ntile - g * 8)
            psum = ps_trA if g == 0 else ps_trB
            if sPE.n >= 1:
                P.wait("tensor", sPE, sPE.n)

            def _ptr(e, g=g, n_in_g=n_in_g, psum=psum, s=sPT.h):
                for j in range(n_in_g):
                    si = g * 8 + j
                    ins = e.transpose(psum[:, j * 128:(j + 1) * 128],
                                      p_tm[:, si * 128:si * 128 + 128],
                                      P.ident_sb[:, :])
                ins.then_inc(s, 1)
            P.on("tensor", _ptr)
            vPT = sPT.next(1)
            P.wait("vector", sPT, vPT)
            sPE.next(1)
            P.on("vector", lambda e, g=g, n_in_g=n_in_g, psum=psum,
                 s=sPE.h: e.tensor_copy(
                     ap(pt_sb, g * 1024, [[1, n_in_g * 128]]),
                     psum[:, 0:n_in_g * 128]).then_inc(s, 1))

        cum_groups += ngr
        P.wait("tensor", sPE, sPE.n)
        P.wait("tensor", sADD, qi)

        def _av(e, qi=qi, ntile=ntile, s=sAV.h):
            for si in range(ntile):
                for nc2 in range(2):
                    ins = e.matmul(ps_av[:, nc2 * 512:(nc2 + 1) * 512],
                                   ap(pt_sb, si * 128, [[1, 128]]),
                                   ap(v_tm, si * D + nc2 * 512, [[1, 512]]),
                                   start=(si == 0), stop=(si == ntile - 1))
            ins.then_inc(s, 1)
        P.on("tensor", _av)
        vAV = sAV.next(1)
        P.wait("scalar", sAV, vAV)
        P.wait("scalar", sR, vR)
        vAVE = sAVE.next(1)
        P.on("scalar", lambda e, s=sAVE.h: e.activation(
            av_tm[:, :], ps_av[:, :], AF.Identity,
            scale=rinv[:, 0:1]).then_inc(s, 1))
        P.wait("tensor", sAVE, vAVE)

        def _avt(e, s=sAT.h):
            for c in range(KC):
                ins = e.transpose(ps_trA[:, c * 128:(c + 1) * 128],
                                  av_tm[:, c * 128:(c + 1) * 128],
                                  P.ident_sb[:, :])
            ins.then_inc(s, 1)
        P.on("tensor", _avt)
        vAT = sAT.next(1)
        P.wait("vector", sAT, vAT)
        vATE = sATE.next(1)
        P.on("vector", lambda e, s=sATE.h: e.tensor_copy(
            av_fm[:, :], ps_trA[:, :]).then_inc(s, 1))
        P.wait("tensor", sATE, vATE)

        def _wo(e, s=sWO.h):
            for nc2 in range(2):
                for kc in range(KC):
                    ins = e.matmul(ps_av[:, nc2 * 512:(nc2 + 1) * 512],
                                   av_fm[:, kc * 128:(kc + 1) * 128],
                                   w_sb[0][:, kc * D + nc2 * 512:
                                           kc * D + nc2 * 512 + 512],
                                   start=(kc == 0), stop=(kc == KC - 1))
            ins.then_inc(s, 1)
        P.on("tensor", _wo)
        vWO = sWO.next(1)
        par = qi % 2
        if qi >= 2:
            P.wait("sync", s_st[par], 16 * (qi // 2))
        vxa = s_xa[par].next(16)
        P.on("sync", lambda e, qi=qi, s=s_xa[par].h: e.dma_start(
            out=xadd[qi % 2][:, :],
            in_=x_in.rearrange("(n p) d -> n p d", p=128)[qi]
        ).then_inc(s, 16))
        P.wait("vector", sWO, vWO)
        P.wait("vector", s_xa[par], vxa)
        vADD = sADD.next(1)
        P.on("vector", lambda e, qi=qi, s=sADD.h: e.tensor_add(
            xadd[qi % 2][:, :], xadd[qi % 2][:, :],
            ps_av[:, :]).then_inc(s, 1))
        P.wait("sync", sADD, vADD)
        s_st[par].next(16)
        P.on("sync", lambda e, qi=qi, s=s_st[par].h: e.dma_start(
            out=x_out.rearrange("(n p) d -> n p d", p=128)[qi],
            in_=xadd[qi % 2][:, :]).then_inc(s, 16))

    P.wait("sync", s_st[0], s_st[0].n)
    P.wait("sync", s_st[1], s_st[1].n)
    return P



_BUILT = {}


def _get_prog(name, builder):
    if name not in _BUILT:
        P = builder()
        nc = P.finalize()
        _BUILT[name] = (nc, P.consts)
    return _BUILT[name]


def run_l0_attn(x, Wq, Wk, Wv, Wo):
    nc, consts = _get_prog("l0attn", build_l0_attn)
    in_maps = []
    for b in range(8):
        m = {"x": np.ascontiguousarray(x[b]), "Wq": Wq, "Wk": Wk,
             "Wv": Wv, "Wo": Wo}
        for k, v in consts.items():
            m[k] = v
        in_maps.append(m)
    res = run_bass_kernel_spmd(nc, in_maps, core_ids=list(range(8)))
    return np.stack([r["xo"] for r in res.results], axis=0)


def _run8(nc, consts, per_core, outs):
    in_maps = []
    for b in range(8):
        m = dict(per_core[b])
        for k, v in consts.items():
            m[k] = v
        in_maps.append(m)
    res = run_bass_kernel_spmd(nc, in_maps, core_ids=list(range(8)))
    return [np.stack([r[o] for r in res.results], axis=0) for o in outs]


def run_l1_ffn(x, ins, li_weights_idx):
    w = li_weights_idx
    return None


def kernel(**inputs):
    ins = inputs
    x = np.ascontiguousarray(np.asarray(ins["L0_attn_x"], np.float32))

    nc, cc = _get_prog("l0ffn", build_l0_ffn)
    ffn_keys = {"gate_w": "L0_ffn_gate_w", "gate_b": "L0_ffn_gate_b",
                "w1": "L0_ffn_w1", "b1": "L0_ffn_b1", "w3": "L0_ffn_w3",
                "b3": "L0_ffn_b3", "w2": "L0_ffn_w2", "b2": "L0_ffn_b2",
                "sw1": "L0_ffn_sw1", "sb1": "L0_ffn_sb1",
                "sw3": "L0_ffn_sw3", "sb3": "L0_ffn_sb3",
                "sw2": "L0_ffn_sw2", "sb2": "L0_ffn_sb2"}

    def run_l0ffn(xc):
        per = [dict({"x": np.ascontiguousarray(xc[b])},
                    **{k: ins[v] for k, v in ffn_keys.items()})
               for b in range(8)]
        return _run8(nc, cc, per, ["xo"])[0]

    x = run_l0ffn(x)

    nca, cca = _get_prog("l0attn", build_l0_attn)
    per = [{"x": np.ascontiguousarray(x[b]), "Wq": ins["L0_attn_Wq"],
            "Wk": ins["L0_attn_Wk"], "Wv": ins["L0_attn_Wv"],
            "Wo": ins["L0_attn_Wo"]} for b in range(8)]
    x = _run8(nca, cca, per, ["xo"])[0]

    x = run_l0ffn(x)

    ncat, ccat = _get_prog("l1aT", lambda: build_l1_layer_attn(True))
    ncaf, ccaf = _get_prog("l1aF", lambda: build_l1_layer_attn(False))
    ncmf, ccmf = _get_prog("l1mF", lambda: build_l1_layer_mlp(False))
    ncmt, ccmt = _get_prog("l1mT", lambda: build_l1_layer_mlp(True))

    def run_l1ffn(xc):
        cur = xc
        for li in range(2):
            nc_, cc_ = (ncat, ccat) if li == 0 else (ncaf, ccaf)
            per = [{"x": np.ascontiguousarray(cur[b]),
                    "wqkv": ins["L1_ffn_wqkv"][li],
                    "q_scale": ins["L1_ffn_q_scale"][li],
                    "k_scale": ins["L1_ffn_k_scale"][li],
                    "wo": ins["L1_ffn_wo"][li]} for b in range(8)]
            cur = _run8(nc_, cc_, per, ["xo"])[0]
            fin = (li == 1)
            nc2_, cc2_ = (ncmt, ccmt) if fin else (ncmf, ccmf)
            per = [dict({"x": np.ascontiguousarray(cur[b]),
                         "ln_g": ins["L1_ffn_ln_g"][li],
                         "ln_b": ins["L1_ffn_ln_b"][li],
                         "w1": ins["L1_ffn_w1"][li],
                         "b1": ins["L1_ffn_b1"][li],
                         "w2": ins["L1_ffn_w2"][li],
                         "b2": ins["L1_ffn_b2"][li]},
                        **({"x2": np.ascontiguousarray(xc[b])} if fin
                           else {})) for b in range(8)]
            cur = _run8(nc2_, cc2_, per, ["xo"])[0]
        return cur

    x = run_l1ffn(x)

    ncm1, ccm1 = _get_prog("m1", build_m1)
    per = [{"x": np.ascontiguousarray(x[b]),
            "in_proj_w": ins["L1_attn_in_proj_w"],
            "conv_weight": ins["L1_attn_conv_weight"],
            "conv_bias": ins["L1_attn_conv_bias"]} for b in range(8)]
    xbc, z, dtx = _run8(ncm1, ccm1, per, ["xbc", "z", "dt"])
    ncm2, ccm2 = _get_prog("m2", build_m2)
    per = [{"x": np.ascontiguousarray(x[b]), "xbc": xbc[b], "z": z[b],
            "dt": dtx[b], "dt_bias": ins["L1_attn_dt_bias"],
            "A_log": ins["L1_attn_A_log"],
            "D_param": ins["L1_attn_D_param"],
            "norm_weight": ins["L1_attn_norm_weight"],
            "out_proj_w": ins["L1_attn_out_proj_w"]} for b in range(8)]
    x = _run8(ncm2, ccm2, per, ["xo"])[0]

    x = run_l1ffn(x)
    return np.ascontiguousarray(x.astype(np.float32))



INTER = 1408
HC = 11
NE = 16
NES = 18


def build_l0_ffn(dbg=False):
    P = Prog("l0ffn")
    x_in = P.dram_in("x", [T, D])
    gw_d = P.dram_in("gate_w", [D, NE])
    gb_d = P.dram_in("gate_b", [NE])
    w1_d = P.dram_in("w1", [NE, D, INTER])
    b1_d = P.dram_in("b1", [NE, INTER])
    w3_d = P.dram_in("w3", [NE, D, INTER])
    b3_d = P.dram_in("b3", [NE, INTER])
    w2_d = P.dram_in("w2", [NE, INTER, D])
    b2_d = P.dram_in("b2", [NE, D])
    sw1_d = P.dram_in("sw1", [D, 2 * INTER])
    sb1_d = P.dram_in("sb1", [2 * INTER])
    sw3_d = P.dram_in("sw3", [D, 2 * INTER])
    sb3_d = P.dram_in("sb3", [2 * INTER])
    sw2_d = P.dram_in("sw2", [2 * INTER, D])
    sb2_d = P.dram_in("sb2", [D])
    x_out = P.dram_out("xo", [T, D])

    cc = host_consts_common()
    ident_d = P.const_in("c_ident", cc["ident"])
    identf_d = P.const_in("c_identf", np.eye(128, dtype=np.float32))
    ones_d = P.const_in("c_ones", np.ones((1, 128), dtype=ml_dtypes.bfloat16))
    half_d = P.const_in("c_half",
                        np.full((1, 128), 0.5, dtype=ml_dtypes.bfloat16))

    P.ident_sb = P.sbuf("ident", [128, 128], BF16)
    identf_sb = P.sbuf("identf", [128, 128], F32)
    ones_sb = P.sbuf("ones", [1, 128], BF16)
    half_sb = P.sbuf("half", [1, 128], BF16)
    P.ln_scratch = (P.sbuf("lnjunk", [128, D], BF16),
                    [P.sbuf(f"lntm{j}", [128, D], BF16) for j in range(2)])
    xacc = P.sbuf("xacc", [128, NT * D], F32)
    xn_fm = P.sbuf("xn_fm", [128, KC * T], BF16)
    w1b = [P.sbuf("w1b0", [128, KC * INTER], BF16)]
    w3b = P.sbuf("w3b", [128, KC * INTER], BF16)
    w2b = P.sbuf("w2b", [128, HC * D], BF16)
    a_fm = P.sbuf("a_fm", [128, HC * 512], BF16)
    sh_sb = [P.sbuf(f"sh{j}", [128, 512], BF16) for j in range(2)]
    b1cols = P.sbuf("b1cols", [128, NES * HC], F32)
    b3cols = P.sbuf("b3cols", [128, NES * HC], F32)
    brow = [P.sbuf("brow0", [1, INTER], F32)]
    gate_sb = P.sbuf("gate_sb", [128, KC * NE], BF16)
    gb_row = P.sbuf("gb_row", [1, NE], BF16)
    b2_sb = P.sbuf("b2_sb", [16, D], BF16)
    sb2row = P.sbuf("sb2row", [1, D], BF16)
    lg_all = P.sbuf("lg_all", [128, NT * NE], F32)
    g_m1 = P.sbuf("g_m1", [128, NT], F32)
    g_sum = P.sbuf("g_sum", [128, NT], F32)
    g_sc = P.sbuf("g_sc", [128, NT * NE], F32)
    g_t1 = P.sbuf("g_t1", [128, NT * NE], F32)
    cw16 = P.sbuf("cw16", [128, NT * NE], F32)
    halfcol = P.sbuf("halfcol", [128, 1], F32)
    cwT = P.sbuf("cwT", [16, NT * 128], BF16)
    cwbf = P.sbuf("cwbf", [128, NT * NE], BF16)

    ps_h = P.psum("ps_h", [128, 512], F32)
    ps_g = P.psum("ps_g", [128, 512], F32)
    ps_y = [P.psum(f"ps_y{j}", [128, D], F32) for j in range(2)]
    ps_trA = P.psum("ps_trA", [128, 1024], BF16)
    ps_trB = P.psum("ps_trB", [128, 1024], BF16)

    s_x = P.sem("s_x")
    s_c = P.sem("s_c")
    s_br = P.sem("s_br")
    s_bt = P.sem("s_bt")
    s_be = P.sem("s_be")
    s_w1 = P.sem("s_w1")
    s_w3 = P.sem("s_w3")
    s_w2 = P.sem("s_w2")
    s_gmm = P.sem("s_gmm")
    s_gev = P.sem("s_gev")
    s_gate = P.sem("s_gate")
    s_hf = P.sem("s_hf")
    s_he = P.sem("s_he")
    s_gf = P.sem("s_gf")
    s_ae = P.sem("s_ae")
    s_yf = P.sem("s_yf")
    s_xu = P.sem("s_xu")
    s_cwt = P.sem("s_cwt")
    s_cwe = P.sem("s_cwe")
    s_bf = P.sem("s_bf")
    s_st = [P.sem("st0"), P.sem("st1")]

    vx = s_x.next(16)
    P.on("sync", lambda e, s=s_x.h: e.dma_start(
        out=xacc[:, :], in_=x_in.rearrange("(n p) d -> p n d", p=128)
    ).then_inc(s, 16))
    n_c = 0
    for dst, src in ((P.ident_sb, ident_d), (identf_sb, identf_d),
                     (ones_sb, ones_d), (half_sb, half_d)):
        s_c.next(16)
        n_c += 16
        P.on("sync", lambda e, dst=dst, src=src, s=s_c.h: e.dma_start(
            out=dst[:, :], in_=src[:, :]).then_inc(s, 16))
    vc = n_c
    s_cg = P.sem("s_cg")
    for dst, src in ((gate_sb, gw_d.rearrange("(c p) n -> p c n", p=128)),
                     (gb_row, gb_d),
                     (b2_sb, b2_d[:, :]),
                     (sb2row, sb2_d[None, :])):
        s_cg.next(16)
        P.on("gpsimd", lambda e, dst=dst, src=src, s=s_cg.h: e.dma_start(
            out=dst[:, :], in_=src).then_inc(s, 16))
    vcg = s_cg.n

    rows = []
    for ee in range(NE):
        rows.append((b1_d[ee], b1cols, ee))
    rows.append((sb1_d[0:INTER], b1cols, 16))
    rows.append((sb1_d[INTER:2 * INTER], b1cols, 17))
    for ee in range(NE):
        rows.append((b3_d[ee], b3cols, ee))
    rows.append((sb3_d[0:INTER], b3cols, 16))
    rows.append((sb3_d[INTER:2 * INTER], b3cols, 17))
    P.wait("tensor", s_c, vc)
    for j, (src, dstcols, ee) in enumerate(rows):
        buf = brow[0]
        reg = (j % 2) * 16
        if j >= 1:
            P.wait("sync", s_bt, j)
        vbr = s_br.next(16)
        P.on("sync", lambda e, src=src, buf=buf, s=s_br.h: e.dma_start(
            out=buf[0:1, :], in_=src).then_inc(s, 16))
        P.wait("tensor", s_br, vbr)
        if j >= 2:
            P.wait("tensor", s_be, j - 1)
        vbt = s_bt.next(1)

        def _bt(e, buf=buf, reg=reg, s=s_bt.h):
            for c in range(HC):
                ins = e.transpose(ps_h[:, reg + c:reg + c + 1],
                                  buf[0:1, c * 128:(c + 1) * 128],
                                  identf_sb[0:1, 0:1])
            ins.then_inc(s, 1)
        P.on("tensor", _bt)
        P.wait("vector", s_bt, vbt)
        s_be.next(1)
        P.on("vector", lambda e, dstcols=dstcols, ee=ee, reg=reg,
             s=s_be.h: e.tensor_copy(
                 dstcols[:, ee * HC:(ee + 1) * HC],
                 ps_h[:, reg:reg + HC]).then_inc(s, 1))
    n_bias_tr = s_bt.n

    P.on("vector", lambda e: e.memset(halfcol[:, :], 0.5))

    P.wait("scalar", s_x, vx)
    P.wait("vector", s_x, vx)
    P.wait("tensor", s_c, vc)

    def srcx(i):
        return ap(xacc, i * D, [[1, D]])
    s_fm, s_norm = emit_ln_to_fm(P, srcx, list(range(NT)), xn_fm,
                                 ps_trA, ps_trB, T, 0, "ln")

    P.wait("tensor", s_fm, NT)
    P.wait("tensor", s_bt, n_bias_tr)
    P.wait("tensor", s_cg, vcg)

    def _gmm(e, s=s_gmm.h):
        for i in range(NT):
            for kc in range(KC):
                e.matmul(ps_g[:, i * NE:(i + 1) * NE],
                         ap(xn_fm, kc * T + i * 128, [[1, 128]]),
                         gate_sb[:, kc * NE:(kc + 1) * NE],
                         start=(kc == 0), stop=False)
            ins = e.matmul(ps_g[:, i * NE:(i + 1) * NE],
                           ones_sb[0:1, :], gb_row[0:1, :],
                           start=False, stop=True)
        ins.then_inc(s, 1)
    P.on("tensor", _gmm)
    vgm = s_gmm.next(1)
    P.wait("scalar", s_gmm, vgm)
    vge = s_gev.next(1)
    P.on("scalar", lambda e, s=s_gev.h: e.activation(
        lg_all[:, 0:NT * NE], ps_g[:, 0:NT * NE], AF.Copy).then_inc(s, 1))

    P.wait("vector", s_gev, vge)
    m1ap = ap(g_m1, 0, [[1, NT], [0, NE]])
    sumap = ap(g_sum, 0, [[1, NT], [0, NE]])
    lg3 = ap(lg_all, 0, [[NE, NT], [1, NE]])
    sc3 = ap(g_sc, 0, [[NE, NT], [1, NE]])
    v = s_gate.next(1)
    P.on("vector", lambda e, s=s_gate.h: e.reduce_max(
        ap(g_m1, 0, [[1, NT], [1, 1]]), lg3, axis=AX.X).then_inc(s, 1))
    P.wait("vector", s_gate, v)
    v = s_gate.next(1)
    P.on("vector", lambda e, s=s_gate.h: e.tensor_sub(
        lg_all[:, :], lg_all[:, :], m1ap).then_inc(s, 1))
    P.wait("scalar", s_gate, v)
    v = s_gate.next(1)
    P.on("scalar", lambda e, s=s_gate.h: e.activation(
        g_sc[:, :], lg_all[:, :], AF.Exp).then_inc(s, 1))
    P.wait("vector", s_gate, v)
    v = s_gate.next(1)
    P.on("vector", lambda e, s=s_gate.h: e.reduce_sum(
        ap(g_sum, 0, [[1, NT], [1, 1]]), sc3, axis=AX.X).then_inc(s, 1))
    P.wait("vector", s_gate, v)
    v = s_gate.next(1)
    P.on("vector", lambda e, s=s_gate.h: e.reciprocal(
        g_sum[:, :], g_sum[:, :]).then_inc(s, 1))
    P.wait("vector", s_gate, v)
    v = s_gate.next(1)
    P.on("vector", lambda e, s=s_gate.h: e.tensor_mul(
        g_sc[:, :], g_sc[:, :], sumap).then_inc(s, 1))
    P.wait("vector", s_gate, v)
    v = s_gate.next(1)
    P.on("vector", lambda e, s=s_gate.h: e.reduce_max(
        ap(g_m1, 0, [[1, NT], [1, 1]]), sc3, axis=AX.X).then_inc(s, 1))
    P.wait("vector", s_gate, v)
    v = s_gate.next(1)
    P.on("vector", lambda e, s=s_gate.h: e.tensor_tensor(
        g_t1[:, :], g_sc[:, :], m1ap, ALU.is_equal).then_inc(s, 1))
    P.wait("vector", s_gate, v)
    v = s_gate.next(1)
    P.on("vector", lambda e, s=s_gate.h: e.scalar_tensor_tensor(
        lg_all[:, :], g_t1[:, :], -2.0, g_sc[:, :],
        ALU.mult, ALU.add).then_inc(s, 1))
    P.wait("vector", s_gate, v)
    v = s_gate.next(1)
    P.on("vector", lambda e, s=s_gate.h: e.reduce_max(
        ap(g_sum, 0, [[1, NT], [1, 1]]),
        ap(lg_all, 0, [[NE, NT], [1, NE]]), axis=AX.X).then_inc(s, 1))
    P.wait("vector", s_gate, v)
    v = s_gate.next(1)
    P.on("vector", lambda e, s=s_gate.h: e.tensor_mul(
        g_t1[:, :], g_t1[:, :], m1ap).then_inc(s, 1))
    P.wait("vector", s_gate, v)
    v = s_gate.next(1)
    P.on("vector", lambda e, s=s_gate.h: e.tensor_tensor(
        g_sc[:, :], lg_all[:, :], sumap, ALU.is_equal).then_inc(s, 1))
    P.wait("vector", s_gate, v)
    v = s_gate.next(1)
    P.on("vector", lambda e, s=s_gate.h: e.tensor_mul(
        g_sc[:, :], g_sc[:, :], sumap).then_inc(s, 1))
    P.wait("vector", s_gate, v)
    v = s_gate.next(1)
    P.on("vector", lambda e, s=s_gate.h: e.tensor_add(
        cw16[:, :], g_t1[:, :], g_sc[:, :]).then_inc(s, 1))
    P.wait("vector", s_gate, v)
    v = s_gate.next(1)
    P.on("vector", lambda e, s=s_gate.h: e.tensor_scalar(
        cw16[:, :], cw16[:, :], 0.5, None, ALU.mult).then_inc(s, 1))
    P.wait("vector", s_gate, v)
    vcw_bf = s_gate.next(1)
    P.on("vector", lambda e, s=s_gate.h: e.tensor_copy(
        cwbf[:, :], cw16[:, :]).then_inc(s, 1))
    v_gate_done = s_gate.n

    if dbg:
        dbg_cw = P.dram_out("dbg_cw", [128, NT * NE])
        dbg_xn = P.dram_out("dbg_xn", [128, 1024])
        s_dbg = P.sem("s_dbg")
        s_dbg2 = P.sem("s_dbg2")
        P.wait("sync", s_gate, v_gate_done)
        P.on("sync", lambda e, s=s_dbg.h: e.dma_start(
            out=dbg_cw[:, :], in_=cw16[:, :]).then_inc(s, 16))
        P.wait("gpsimd", s_fm, NT)
        P.on("gpsimd", lambda e, s=s_dbg2.h: e.dma_start(
            out=dbg_xn[:, :], in_=xn_fm[:, 0:1024]).then_inc(s, 16))

    def wsrc(mat, ee):
        if mat == "w1":
            return (w1_d[ee] if ee < NE else
                    sw1_d[:, (ee - NE) * INTER:(ee - NE + 1) * INTER]
                    ).rearrange("(c p) n -> p c n", p=128)
        if mat == "w3":
            return (w3_d[ee] if ee < NE else
                    sw3_d[:, (ee - NE) * INTER:(ee - NE + 1) * INTER]
                    ).rearrange("(c p) n -> p c n", p=128)
        return (w2_d[ee] if ee < NE else
                sw2_d[(ee - NE) * INTER:(ee - NE + 1) * INTER, :]
                ).rearrange("(c p) n -> p c n", p=128)

    for ee in range(NES):
        if ee >= 1:
            P.wait("gpsimd", s_hf, 4 * HC * ee)
        s_w1.next(16)
        P.on("gpsimd", lambda e, ee=ee, s=s_w1.h: e.dma_start(
            out=w1b[0][:, :], in_=wsrc("w1", ee)).then_inc(s, 16))
        if ee >= 1:
            P.wait("gpsimd", s_gf, 4 * HC * ee)
        s_w3.next(16)
        P.on("gpsimd", lambda e, ee=ee, s=s_w3.h: e.dma_start(
            out=w3b[:, :], in_=wsrc("w3", ee)).then_inc(s, 16))
        if ee >= 1:
            P.wait("gpsimd", s_yf, 16 * ee)
        s_w2.next(16)
        P.on("gpsimd", lambda e, ee=ee, s=s_w2.h: e.dma_start(
            out=w2b[:, :], in_=wsrc("w2", ee)).then_inc(s, 16))

    P.wait("tensor", s_gev, vge)
    P.wait("tensor", s_be, len(rows))
    P.wait("vector", s_norm, NT)
    P.wait("vector", s_gate, v_gate_done)
    first_y = True
    for ee in range(NES):
        P.wait("tensor", s_w1, 16 * (ee + 1))
        P.wait("tensor", s_w3, 16 * (ee + 1))
        for tq in range(4):
            for hc in range(HC):
                vhf = s_hf.next(1)
                if s_he.n >= 1:
                    P.wait("tensor", s_he, s_he.n)

                def _hf(e, ee=ee, tq=tq, hc=hc, s=s_hf.h):
                    for kc in range(KC):
                        ins = e.matmul(
                            ps_h[:, :],
                            w1b[0][:, kc * INTER + hc * 128:
                                        kc * INTER + hc * 128 + 128],
                            ap(xn_fm, kc * T + tq * 512, [[1, 512]]),
                            start=(kc == 0), stop=(kc == KC - 1))
                    ins.then_inc(s, 1)
                P.on("tensor", _hf)
                P.wait("scalar", s_hf, vhf)
                if s_ae.n >= 2:
                    P.wait("scalar", s_ae, s_ae.n - 1)
                vhe = s_he.next(1)
                P.on("scalar", lambda e, ee=ee, hc=hc, buf=sh_sb[hc % 2],
                     s=s_he.h: e.activation(
                         buf[:, :], ps_h[:, :], AF.Silu,
                         bias=b1cols[:, ee * HC + hc:ee * HC + hc + 1]
                     ).then_inc(s, 1))
                vgf = s_gf.next(1)
                if s_ae.n >= 1:
                    P.wait("tensor", s_ae, s_ae.n)

                def _gf(e, ee=ee, tq=tq, hc=hc, s=s_gf.h):
                    for kc in range(KC):
                        ins = e.matmul(
                            ps_g[:, :],
                            w3b[:, kc * INTER + hc * 128:
                                kc * INTER + hc * 128 + 128],
                            ap(xn_fm, kc * T + tq * 512, [[1, 512]]),
                            start=(kc == 0), stop=(kc == KC - 1))
                    ins.then_inc(s, 1)
                P.on("tensor", _gf)
                P.wait("vector", s_gf, vgf)
                P.wait("vector", s_he, vhe)
                if first_y is False:
                    pass
                if s_yf.n >= 16 or (ee > 0 or tq > 0):
                    P.wait("vector", s_yf, s_yf.n)
                s_ae.next(1)
                P.on("vector", lambda e, ee=ee, hc=hc, buf=sh_sb[hc % 2],
                     s=s_ae.h: e.scalar_tensor_tensor(
                         ap(a_fm, hc * 512, [[1, 512]]), ps_g[:, :],
                         b3cols[:, ee * HC + hc:ee * HC + hc + 1],
                         buf[:, :], ALU.add, ALU.mult).then_inc(s, 1))
            P.wait("tensor", s_ae, s_ae.n)
            P.wait("tensor", s_w2, 16 * (ee + 1))
            for tt in range(4):
                tile = tq * 4 + tt
                psy = ps_y[tt % 2]
                vyf = s_yf.next(1)
                if s_xu.n >= 2:
                    P.wait("tensor", s_xu, s_xu.n - 1)

                def _yf(e, tt=tt, psy=psy, s=s_yf.h):
                    for nc2 in range(2):
                        for hc in range(HC):
                            ins = e.matmul(
                                psy[:, nc2 * 512:(nc2 + 1) * 512],
                                ap(a_fm, hc * 512 + tt * 128, [[1, 128]]),
                                w2b[:, hc * D + nc2 * 512:
                                    hc * D + nc2 * 512 + 512],
                                start=(hc == 0), stop=(hc == HC - 1))
                    ins.then_inc(s, 1)
                P.on("tensor", _yf)
                P.wait("vector", s_yf, vyf)
                cwcol = (ap(cw16, tile * NE + ee, [[1, 1]]) if ee < NE
                         else halfcol[:, 0:1])
                s_xu.next(1)
                P.on("vector", lambda e, tile=tile, psy=psy, cwcol=cwcol,
                     s=s_xu.h: e.scalar_tensor_tensor(
                         ap(xacc, tile * D, [[1, D]]), psy[:, :], cwcol,
                         ap(xacc, tile * D, [[1, D]]),
                         ALU.mult, ALU.add).then_inc(s, 1))

    P.wait("tensor", s_gate, vcw_bf)
    for i in range(NT):
        psum = ps_trA if i % 2 == 0 else ps_trB
        if i >= 2:
            P.wait("tensor", s_cwe, i - 1)
        vt = s_cwt.next(1)
        P.on("tensor", lambda e, i=i, psum=psum, s=s_cwt.h: e.transpose(
            bass.AP(psum.ap().tensor, 0, [[psum.ap().ap[0][0], 16],
                                          [1, 128]]),
            cwbf[:, i * NE:(i + 1) * NE],
            P.ident_sb[:, :]).then_inc(s, 1))
        P.wait("vector", s_cwt, vt)
        s_cwe.next(1)
        P.on("vector", lambda e, i=i, psum=psum, s=s_cwe.h: e.tensor_copy(
            bass.AP(cwT.ap().tensor, i * 128,
                    [[cwT.ap().ap[0][0], 16], [1, 128]]),
            bass.AP(psum.ap().tensor, 0,
                    [[psum.ap().ap[0][0], 16], [1, 128]])).then_inc(s, 1))

    P.wait("tensor", s_cwe, s_cwe.n)
    for i in range(NT):
        tile = i
        psy = ps_y[i % 2]
        vbf = s_bf.next(1)
        P.wait("tensor", s_xu, s_xu.n)

        def _b2(e, i=i, psy=psy, s=s_bf.h):
            for nc2 in range(2):
                e.matmul(psy[:, nc2 * 512:(nc2 + 1) * 512],
                         bass.AP(cwT.ap().tensor, i * 128,
                                 [[cwT.ap().ap[0][0], 16], [1, 128]]),
                         b2_sb[:, nc2 * 512:(nc2 + 1) * 512],
                         start=True, stop=False)
                ins = e.matmul(psy[:, nc2 * 512:(nc2 + 1) * 512],
                               half_sb[0:1, :],
                               sb2row[0:1, nc2 * 512:(nc2 + 1) * 512],
                               start=False, stop=True)
            ins.then_inc(s, 1)
        P.on("tensor", _b2)
        P.wait("vector", s_bf, vbf)
        vxu = s_xu.next(1)
        P.on("vector", lambda e, i=i, psy=psy, s=s_xu.h: e.tensor_add(
            ap(xacc, i * D, [[1, D]]), ap(xacc, i * D, [[1, D]]),
            psy[:, :]).then_inc(s, 1))
        par = i % 2
        P.wait("sync", s_xu, vxu)
        s_st[par].next(16)
        P.on("sync", lambda e, i=i, s=s_st[par].h: e.dma_start(
            out=x_out.rearrange("(n p) d -> n p d", p=128)[i],
            in_=ap(xacc, i * D, [[1, D]])).then_inc(s, 16))

    P.wait("sync", s_st[0], s_st[0].n)
    P.wait("sync", s_st[1], s_st[1].n)
    return P


def run_l0_ffn(x, ins):
    nc, consts = _get_prog("l0ffn", build_l0_ffn)
    in_maps = []
    for b in range(8):
        m = {"x": np.ascontiguousarray(x[b]),
             "gate_w": ins["L0_ffn_gate_w"], "gate_b": ins["L0_ffn_gate_b"],
             "w1": ins["L0_ffn_w1"], "b1": ins["L0_ffn_b1"],
             "w3": ins["L0_ffn_w3"], "b3": ins["L0_ffn_b3"],
             "w2": ins["L0_ffn_w2"], "b2": ins["L0_ffn_b2"],
             "sw1": ins["L0_ffn_sw1"], "sb1": ins["L0_ffn_sb1"],
             "sw3": ins["L0_ffn_sw3"], "sb3": ins["L0_ffn_sb3"],
             "sw2": ins["L0_ffn_sw2"], "sb2": ins["L0_ffn_sb2"]}
        for k, v in consts.items():
            m[k] = v
        in_maps.append(m)
    res = run_bass_kernel_spmd(nc, in_maps, core_ids=list(range(8)))
    return np.stack([r["xo"] for r in res.results], axis=0)



HEADS = 8
DH = 64
WIN = 512
INNER8 = 512
QKS = 8.0


def host_rotary_consts():
    n = T
    half = np.arange(0, DH, 2, dtype=np.float64)
    inv_freq = 1.0 / 10000.0 ** (half / DH)
    t = np.arange(n, dtype=np.float64)
    freqs = t[:, None] * inv_freq[None, :]
    freqs = np.concatenate([freqs, freqs], axis=-1)
    base = (half + 0.4 * DH) / (1.4 * DH)
    power = (t - n // 2) / float(WIN // 2)
    scale = base[None, :] ** power[:, None]
    scale = np.concatenate([scale, scale], axis=-1)
    cos, sin = np.cos(freqs), np.sin(freqs)
    sgn = np.concatenate([-np.ones(32), np.ones(32)])[None, :]
    tabs = {"CSq": cos * scale, "SNq": sin * scale * sgn,
            "CSk": cos / scale, "SNk": sin / scale * sgn}
    out = {}
    for k, v in tabs.items():
        v = v.astype(np.float32)
        tm = np.zeros((128, NT * DH), np.float32)
        for i in range(NT):
            tm[:, i * DH:(i + 1) * DH] = v[i * 128:(i + 1) * 128]
        out[k] = tm.astype(ml_dtypes.bfloat16)
    return out


def build_l1_layer_attn(ln_input, dbg=False):
    P = Prog("l1a")
    x_in = P.dram_in("x", [T, D])
    wqkv_d = P.dram_in("wqkv", [D, 3 * INNER8])
    qs_d = P.dram_in("q_scale", [DH])
    ks_d = P.dram_in("k_scale", [DH])
    wo_d = P.dram_in("wo", [INNER8, D])
    x_out = P.dram_out("xo", [T, D])

    cc = host_consts_common()
    rt = host_rotary_consts()
    ident_d = P.const_in("c_ident", cc["ident"])
    maskc_d = P.const_in("c_mask", cc["mask_causal"])
    maskl_d = P.const_in("c_maskl", cc["mask_lead"])
    ones_d = P.const_in("c_ones", np.ones((1, 128), dtype=ml_dtypes.bfloat16))
    onesf_d = P.const_in("c_onesf", np.ones((1, 128), dtype=np.float32))
    rot_d = {k: P.const_in("c_" + k, rt[k]) for k in rt}

    P.ident_sb = P.sbuf("ident", [128, 128], BF16)
    maskc_sb = P.sbuf("maskc", [128, 128], F32)
    maskl_sb = P.sbuf("maskl", [128, 128], F32)
    ones_sb = P.sbuf("ones", [1, 128], BF16)
    onesf_sb = P.sbuf("onesf", [1, 128], F32)
    rot_sb = {k: P.sbuf("rt" + k, [128, NT * DH], BF16) for k in rt}
    P.ln_scratch = (P.sbuf("lnjunk", [128, D], BF16),
                    [P.sbuf(f"lntm{j}", [128, D], BF16) for j in range(2)])
    xi = P.sbuf("xi", [128, NT * D], F32)
    bkf = P.sbuf("bkf", [64, HEADS * T], BF16)
    v_tm = P.sbuf("v_tm", [128, NT * INNER8], BF16)
    wqkv_sb = P.sbuf("wqkv_sb", [128, KC * 3 * INNER8], BF16)
    wo_sb = P.sbuf("wo_sb", [128, 4 * D], BF16)
    qsrow = P.sbuf("qsrow", [1, DH], F32)
    ksrow = P.sbuf("ksrow", [1, DH], F32)
    qs_b = P.sbuf("qs_b", [128, DH], F32)
    ks_b = P.sbuf("ks_b", [128, DH], F32)
    xfm = [P.sbuf(f"xfm{j}", [128, KC * 132], BF16) for j in range(2)]
    qk_tm = P.sbuf("qk_tm", [128, 2 * INNER8], F32)
    qss = P.sbuf("qss", [128, 2 * HEADS], F32)
    bq_t = P.sbuf("bq_t", [64, HEADS * 128], BF16)
    p_tm = P.sbuf("p_tm", [128, HEADS * 640], BF16)
    pt_sb = P.sbuf("pt_sb", [128, HEADS * 640], BF16)
    nmax = P.sbuf("nmax", [128, HEADS], F32)
    rsum = P.sbuf("rsum", [128, HEADS], F32)
    rinv = P.sbuf("rinv", [128, HEADS], F32)
    o_tm = P.sbuf("o_tm", [128, INNER8], BF16)
    o_fm = P.sbuf("o_fm", [128, 4 * 128], BF16)
    t1_sb = P.sbuf("t1_sb", [128, INNER8], F32)
    t2_sb = P.sbuf("t2_sb", [128, INNER8], F32)
    t3_sb = P.sbuf("t3_sb", [128, INNER8], F32)
    xcast = P.sbuf("xcast", [128, D], BF16)

    ps_qkv = P.psum("ps_qkv", [128, INNER8], F32)
    ps_sim = P.psum("ps_sim", [128, 2 * 640], F32)
    ps_tr = P.psum("ps_tr", [128, 1536], BF16)
    ps_o = P.psum("ps_o", [128, 1024], F32)

    s_x = P.sem("s_x")
    s_c = P.sem("s_c")
    s_w = P.sem("s_w")
    s_st = [P.sem("st0"), P.sem("st1")]

    vx = s_x.next(16)
    P.on("sync", lambda e, s=s_x.h: e.dma_start(
        out=xi[:, :], in_=x_in.rearrange("(n p) d -> p n d", p=128)
    ).then_inc(s, 16))
    nc_ = 0
    loads = [(P.ident_sb, ident_d), (maskc_sb, maskc_d), (maskl_sb, maskl_d),
             (ones_sb, ones_d), (onesf_sb, onesf_d)] + [
                 (rot_sb[k], rot_d[k]) for k in rot_sb]
    for dst, src in loads:
        s_c.next(16)
        nc_ += 16
        P.on("sync", lambda e, dst=dst, src=src, s=s_c.h: e.dma_start(
            out=dst[:, :], in_=src[:, :]).then_inc(s, 16))
    for dst, src in ((qsrow, qs_d), (ksrow, ks_d)):
        s_c.next(16)
        nc_ += 16
        P.on("sync", lambda e, dst=dst, src=src, s=s_c.h: e.dma_start(
            out=dst[0:1, :], in_=src).then_inc(s, 16))
    vc = nc_
    s_cg = P.sem("s_cg")
    for dst, src in ((wqkv_sb, wqkv_d.rearrange("(c p) n -> p c n", p=128)),
                     (wo_sb, wo_d.rearrange("(c p) n -> p c n", p=128))):
        s_cg.next(16)
        P.on("gpsimd", lambda e, dst=dst, src=src, s=s_cg.h: e.dma_start(
            out=dst[:, :], in_=src).then_inc(s, 16))
    vcg = s_cg.n

    s_qsb = P.sem("s_qsb")
    P.wait("tensor", s_c, vc)
    v1 = s_qsb.next(2)
    P.on("tensor", lambda e, s=s_qsb.h: e.matmul(
        ps_o[:, 0:DH], onesf_sb[0:1, :], qsrow[0:1, :],
        start=True, stop=True).then_inc(s, 1))
    P.on("tensor", lambda e, s=s_qsb.h: e.matmul(
        ps_o[:, 64:64 + DH], onesf_sb[0:1, :], ksrow[0:1, :],
        start=True, stop=True).then_inc(s, 1))
    P.wait("vector", s_qsb, v1)
    vqsb = s_qsb.next(1)
    P.on("vector", lambda e, s=s_qsb.h: e.tensor_scalar(
        qs_b[:, :], ps_o[:, 0:DH], QKS, None, ALU.mult).then_inc(s, 1))
    vksb = s_qsb.next(1)
    P.on("vector", lambda e, s=s_qsb.h: e.tensor_copy(
        ks_b[:, :], ps_o[:, 64:64 + DH]).then_inc(s, 1))

    if ln_input:
        def srcx(i):
            return ap(xi, i * D, [[1, D]])
        P.wait("scalar", s_x, vx)
        P.wait("vector", s_x, vx)
        s_lnw = emit_ln_inplace(P, srcx, "ln")
    else:
        s_lnw = None

    s_tri = P.sem("s_tri")
    s_col = P.sem("s_col")
    s_qkvf = P.sem("s_qkvf")
    s_qke = P.sem("s_qke")
    s_rot = P.sem("s_rot")
    s_bqt = P.sem("s_bqt")
    s_bqe = P.sem("s_bqe")
    s_sf = P.sem("s_sf")
    s_msk = P.sem("s_msk")
    s_exp = P.sem("s_exp")
    s_ptr = P.sem("s_ptr")
    s_pte = P.sem("s_pte")
    s_pv = P.sem("s_pv")
    s_pve = P.sem("s_pve")
    s_otr = P.sem("s_otr")
    s_ote = P.sem("s_ote")
    s_wof = P.sem("s_wof")
    s_xu = P.sem("s_xu")
    s_rin = P.sem("s_rin")

    P.wait("tensor", s_cg, vcg)
    if ln_input:
        P.wait("tensor", s_lnw, NT)

    s_xc = P.sem("s_xc")
    for i in range(NT):
        buf = xfm[i % 2]
        P.wait("scalar", s_x, vx)
        if i >= 1:
            P.wait("scalar", s_tri, i)
        vxc = s_xc.next(1)
        P.on("scalar", lambda e, i=i, s=s_xc.h: e.activation(
            xcast[:, :], ap(xi, i * D, [[1, D]]), AF.Copy).then_inc(s, 1))
        P.wait("tensor", s_xc, vxc)
        if i >= 2:
            P.wait("tensor", s_qkvf, 3 * (i - 1))
        vtri = s_tri.next(1)

        def _tri(e, i=i, buf=buf, s=s_tri.h):
            for c in range(KC):
                ins = e.transpose(
                    ps_tr[:, c * 128:(c + 1) * 128],
                    xcast[:, c * 128:(c + 1) * 128],
                    P.ident_sb[:, :])
            ins.then_inc(s, 1)
        P.on("tensor", _tri)
        P.wait("vector", s_tri, vtri)
        vcol = s_col.next(1)
        prev = xfm[(i + 1) % 2]

        def _col(e, i=i, buf=buf, prev=prev, s=s_col.h):
            e.tensor_copy(ap(buf, 1, [[132, KC], [1, 128]]), ps_tr[:, 0:1024])
            if i == 0:
                e.memset(ap(buf, 0, [[132, KC], [1, 1]]), 0.0)
            else:
                e.tensor_copy(ap(buf, 0, [[132, KC], [1, 1]]),
                              ap(prev, 128, [[132, KC], [1, 1]]))
            e.engine_nop().then_inc(s, 1)
        P.on("vector", _col)
        P.wait("tensor", s_col, vcol)
        for nc3 in range(3):
            if s_qke.n >= 1:
                P.wait("tensor", s_qke, s_qke.n)
            vqf = s_qkvf.next(1)

            def _qkv(e, i=i, buf=buf, nc3=nc3, s=s_qkvf.h):
                for kc in range(KC):
                    off = 1 if kc < 4 else 0
                    ins = e.matmul(
                        ps_qkv[:, :],
                        ap(buf, kc * 132 + off, [[1, 128]]),
                        wqkv_sb[:, kc * 1536 + nc3 * 512:
                                kc * 1536 + nc3 * 512 + 512],
                        start=(kc == 0), stop=(kc == KC - 1))
                ins.then_inc(s, 1)
            P.on("tensor", _qkv)
            P.wait("scalar", s_qkvf, vqf)
            if nc3 < 2 and i >= 1:
                P.wait("scalar", s_rot, 10 * i)
            s_qke.next(1)
            if nc3 < 2:
                P.on("scalar", lambda e, nc3=nc3, s=s_qke.h: e.activation(
                    qk_tm[:, nc3 * 512:(nc3 + 1) * 512], ps_qkv[:, :],
                    AF.Copy).then_inc(s, 1))
            else:
                P.on("scalar", lambda e, i=i, s=s_qke.h: e.activation(
                    ap(v_tm, i * INNER8, [[1, INNER8]]), ps_qkv[:, :],
                    AF.Copy).then_inc(s, 1))
        vqe = s_qke.n
        P.wait("vector", s_qke, vqe)
        rr = s_rot.n
        s_rot.next(18)
        for qk in range(2):
            base = qk * 512
            P.on("vector", lambda e, base=base, s=s_rot.h: e.tensor_mul(
                t1_sb[:, :], ap(qk_tm, base, [[1, 512]]),
                ap(qk_tm, base, [[1, 512]])).then_inc(s, 1))
            P.wait("vector", s_rot, rr + 2 * qk + 1)
            P.on("vector", lambda e, qk=qk, s=s_rot.h: e.reduce_sum(
                ap(qss, qk * HEADS, [[1, HEADS], [0, 1]]),
                bass.AP(t1_sb.ap().tensor, 0,
                        [t1_sb.ap().ap[0], [64, HEADS], [1, 64]]),
                axis=AX.X).then_inc(s, 1))
            if qk == 0:
                P.wait("vector", s_rot, rr + 2)
        P.wait("scalar", s_rot, rr + 4)
        vsq = s_qke.next(1)
        P.on("scalar", lambda e, s=s_qke.h: e.activation(
            qss[:, :], qss[:, :], AF.Sqrt).then_inc(s, 1))
        P.wait("vector", s_qke, vsq)
        P.on("vector", lambda e, s=s_rot.h: e.tensor_scalar(
            qss[:, :], qss[:, :], 1e-12, None, ALU.max).then_inc(s, 1))
        P.wait("vector", s_rot, rr + 5)
        P.on("vector", lambda e, s=s_rot.h: e.reciprocal(
            qss[:, :], qss[:, :]).then_inc(s, 1))
        P.wait("vector", s_rot, rr + 6)
        nrot = rr + 6
        for qk in range(2):
            base = qk * 512
            scl = qs_b if qk == 0 else ks_b
            CS = rot_sb["CSq" if qk == 0 else "CSk"]
            SN = rot_sb["SNq" if qk == 0 else "SNk"]
            P.on("vector", lambda e, base=base, qk=qk, s=s_rot.h:
                 e.tensor_tensor(
                     t1_sb[:, :], ap(qk_tm, base, [[1, 512]]),
                     ap(qss, qk * HEADS, [[1, HEADS], [0, DH]]),
                     ALU.mult).then_inc(s, 1))
            P.wait("vector", s_rot, nrot + 1)
            P.on("vector", lambda e, scl=scl, s=s_rot.h: e.tensor_tensor(
                t2_sb[:, :], t1_sb[:, :],
                bass.AP(scl.ap().tensor, 0, [scl.ap().ap[0], [0, HEADS],
                                             [1, DH]]),
                ALU.mult).then_inc(s, 1))
            P.wait("vector", s_rot, nrot + 2)
            P.on("vector", lambda e, CS=CS, i=i, s=s_rot.h: e.tensor_tensor(
                t1_sb[:, :], t2_sb[:, :],
                bass.AP(CS.ap().tensor, i * DH,
                        [CS.ap().ap[0], [0, HEADS], [1, DH]]),
                ALU.mult).then_inc(s, 1))
            P.wait("vector", s_rot, nrot + 3)
            P.on("vector", lambda e, SN=SN, i=i, s=s_rot.h:
                 e.scalar_tensor_tensor(
                     bass.AP(t3_sb.ap().tensor, 0,
                             [t3_sb.ap().ap[0], [64, HEADS], [1, 32]]),
                     bass.AP(t2_sb.ap().tensor, 32,
                             [t2_sb.ap().ap[0], [64, HEADS], [1, 32]]),
                     1.0,
                     bass.AP(SN.ap().tensor, i * DH,
                             [SN.ap().ap[0], [0, HEADS], [1, 32]]),
                     ALU.mult, ALU.mult).then_inc(s, 1))
            P.wait("vector", s_rot, nrot + 4)
            P.on("vector", lambda e, SN=SN, i=i, s=s_rot.h:
                 e.scalar_tensor_tensor(
                     bass.AP(t3_sb.ap().tensor, 32,
                             [t3_sb.ap().ap[0], [64, HEADS], [1, 32]]),
                     bass.AP(t2_sb.ap().tensor, 0,
                             [t2_sb.ap().ap[0], [64, HEADS], [1, 32]]),
                     1.0,
                     bass.AP(SN.ap().tensor, i * DH + 32,
                             [SN.ap().ap[0], [0, HEADS], [1, 32]]),
                     ALU.mult, ALU.mult).then_inc(s, 1))
            P.wait("vector", s_rot, nrot + 5)
            P.on("vector", lambda e, base=base, s=s_rot.h: e.tensor_add(
                ap(qk_tm, base, [[1, 512]]), t1_sb[:, :],
                t3_sb[:, :]).then_inc(s, 1))
            P.wait("vector", s_rot, nrot + 6)
            nrot = nrot + 6
        vrot = s_rot.n
        P.wait("scalar", s_rot, vrot)
        vqe2 = s_qke.next(2)
        P.on("scalar", lambda e, s=s_qke.h: e.activation(
            p_tm[:, 0:512], qk_tm[:, 0:512], AF.Copy).then_inc(s, 1))
        P.on("scalar", lambda e, s=s_qke.h: e.activation(
            p_tm[:, 512:1024], qk_tm[:, 512:1024], AF.Copy).then_inc(s, 1))
        P.wait("tensor", s_qke, vqe2)
        if i >= 1:
            P.wait("tensor", s_bqe, 2 * i)
        vbqt = s_bqt.next(2)

        pstep_tr = ps_tr.ap().ap[0][0]

        def _bqt(e, pstep_tr=pstep_tr, s=s_bqt.h):
            for h in range(HEADS):
                ins = e.transpose(
                    bass.AP(ps_tr.ap().tensor, h * 128,
                            [[pstep_tr, 64], [1, 128]]),
                    p_tm[:, h * 64:h * 64 + 64],
                    P.ident_sb[:, :])
            ins.then_inc(s, 1)

        def _bkt(e, pstep_tr=pstep_tr, s=s_bqt.h):
            for h in range(HEADS):
                ins = e.transpose(
                    bass.AP(ps_tr.ap().tensor, 64 * pstep_tr + h * 128,
                            [[pstep_tr, 64], [1, 128]]),
                    p_tm[:, 512 + h * 64:512 + h * 64 + 64],
                    P.ident_sb[:, :])
            ins.then_inc(s, 1)
        P.on("tensor", _bqt)
        P.on("tensor", _bkt)
        P.wait("vector", s_bqt, vbqt)
        vbqe = s_bqe.next(2)
        P.on("vector", lambda e, pstep_tr=pstep_tr, s=s_bqe.h: e.tensor_copy(
            bass.AP(bq_t.ap().tensor, 0,
                    [[bq_t.ap().ap[0][0], 64], [1, 1024]]),
            bass.AP(ps_tr.ap().tensor, 0,
                    [[pstep_tr, 64], [1, 1024]])).then_inc(s, 1))
        P.on("vector", lambda e, i=i, pstep_tr=pstep_tr,
             s=s_bqe.h: e.tensor_copy(
            bass.AP(bkf.ap().tensor, i * 128,
                    [[bkf.ap().ap[0][0], 64], [T, HEADS], [1, 128]]),
            bass.AP(ps_tr.ap().tensor, 64 * pstep_tr,
                    [[pstep_tr, 64], [1, 1024]])).then_inc(s, 1))

        kts = [s for s in range(max(0, i - 4), i + 1)]
        nkt = len(kts)
        P.wait("tensor", s_bqe, vbqe)
        for hg in range(4):
            h0 = hg * 2
            if s_exp.n >= 1:
                P.wait("tensor", s_exp, s_exp.n)
            vsf = s_sf.next(1)

            def _sim(e, i=i, h0=h0, kts=kts, s=s_sf.h):
                for hh in range(2):
                    h = h0 + hh
                    for j, si in enumerate(kts):
                        ins = e.matmul(
                            ps_sim[:, hh * 640 + j * 128:
                                   hh * 640 + j * 128 + 128],
                            bass.AP(bq_t.ap().tensor, h * 128,
                                    [[bq_t.ap().ap[0][0], 64], [1, 128]]),
                            bass.AP(bkf.ap().tensor, h * T + si * 128,
                                    [[bkf.ap().ap[0][0], 64], [1, 128]]),
                            start=True, stop=True)
                ins.then_inc(s, 1)
            P.on("tensor", _sim)
            P.wait("vector", s_sf, vsf)
            vmsk = s_msk.next(2 + (1 if (i >= 4) else 0))
            if i >= 4:
                P.on("vector", lambda e, s=s_msk.h: e.tensor_tensor(
                    bass.AP(ps_sim.ap().tensor, 0,
                            [ps_sim.ap().ap[0], [640, 2], [1, 128]]),
                    bass.AP(ps_sim.ap().tensor, 0,
                            [ps_sim.ap().ap[0], [640, 2], [1, 128]]),
                    bass.AP(maskl_sb.ap().tensor, 0,
                            [maskl_sb.ap().ap[0], [0, 2], [1, 128]]),
                    ALU.add).then_inc(s, 1))
            dco = (nkt - 1) * 128
            P.on("vector", lambda e, dco=dco, s=s_msk.h: e.tensor_tensor(
                bass.AP(ps_sim.ap().tensor, dco,
                        [ps_sim.ap().ap[0], [640, 2], [1, 128]]),
                bass.AP(ps_sim.ap().tensor, dco,
                        [ps_sim.ap().ap[0], [640, 2], [1, 128]]),
                bass.AP(maskc_sb.ap().tensor, 0,
                        [maskc_sb.ap().ap[0], [0, 2], [1, 128]]),
                ALU.add).then_inc(s, 1))
            wk = nkt * 128
            P.on("vector", lambda e, h0=h0, wk=wk, s=s_msk.h: e.reduce_max(
                bass.AP(nmax.ap().tensor, h0, [nmax.ap().ap[0], [1, 2],
                                               [0, 1]]),
                bass.AP(ps_sim.ap().tensor, 0,
                        [ps_sim.ap().ap[0], [640, 2], [1, wk]]),
                axis=AX.X, negate=True).then_inc(s, 1))
            P.wait("scalar", s_msk, vmsk)
            P.wait("scalar", s_ptr, s_ptr.n)
            vexp = s_exp.next(2)
            for hh in range(2):
                h = h0 + hh
                P.on("scalar", lambda e, h=h, hh=hh, wk=wk,
                     s=s_exp.h: e.activation(
                         ap(p_tm, h * 640, [[1, wk]]),
                         ps_sim[:, hh * 640:hh * 640 + wk], AF.Exp,
                         bias=nmax[:, h:h + 1],
                         accum_out=rsum[:, h:h + 1]).then_inc(s, 1))
            P.wait("tensor", s_exp, vexp)
            if s_pte.n >= 1:
                P.wait("tensor", s_pte, s_pte.n)
            vptr = s_ptr.next(1)

            def _ptr(e, h0=h0, nkt=nkt, s=s_ptr.h):
                for hh in range(2):
                    for j in range(nkt):
                        ins = e.transpose(
                            ps_tr[:, (hh * nkt + j) * 128:
                                  (hh * nkt + j) * 128 + 128],
                            ap(p_tm, (h0 + hh) * 640 + j * 128, [[1, 128]]),
                            P.ident_sb[:, :])
                ins.then_inc(s, 1)
            P.on("tensor", _ptr)
            P.wait("vector", s_ptr, vptr)
            vpte = s_pte.next(1)
            P.on("vector", lambda e, h0=h0, nkt=nkt, s=s_pte.h: e.tensor_copy(
                ap(pt_sb, h0 * 640, [[1, 2 * nkt * 128]]),
                ps_tr[:, 0:2 * nkt * 128]).then_inc(s, 1))
            P.wait("tensor", s_pte, vpte)
            if hg == 0 and s_xu.n >= 1:
                P.wait("tensor", s_xu, s_xu.n)
            vpv = s_pv.next(1)

            def _pv(e, i=i, h0=h0, kts=kts, nkt=nkt, s=s_pv.h):
                for hh in range(2):
                    h = h0 + hh
                    for j, si in enumerate(kts):
                        ins = e.matmul(
                            ps_o[:, h * 64:h * 64 + 64],
                            ap(pt_sb, h0 * 640 + (hh * nkt + j) * 128,
                               [[1, 128]]),
                            ap(v_tm, si * INNER8 + h * 64, [[1, 64]]),
                            start=(j == 0), stop=(j == nkt - 1))
                ins.then_inc(s, 1)
            P.on("tensor", _pv)
            P.wait("vector", s_exp, vexp)
            vrin = s_rin.next(1)
            P.on("vector", lambda e, h0=h0, s=s_rin.h: e.reciprocal(
                rinv[:, h0:h0 + 2], rsum[:, h0:h0 + 2]).then_inc(s, 1))
            P.wait("scalar", s_pv, vpv)
            P.wait("scalar", s_rin, vrin)
            vpve = s_pve.next(2)
            for hh in range(2):
                h = h0 + hh
                P.on("scalar", lambda e, h=h, s=s_pve.h: e.activation(
                    o_tm[:, h * 64:h * 64 + 64],
                    ps_o[:, h * 64:h * 64 + 64], AF.Identity,
                    scale=rinv[:, h:h + 1]).then_inc(s, 1))
        P.wait("tensor", s_pve, s_pve.n)
        votr = s_otr.next(1)

        def _otr(e, s=s_otr.h):
            for c in range(4):
                ins = e.transpose(ps_tr[:, c * 128:(c + 1) * 128],
                                  o_tm[:, c * 128:(c + 1) * 128],
                                  P.ident_sb[:, :])
            ins.then_inc(s, 1)
        P.on("tensor", _otr)
        P.wait("vector", s_otr, votr)
        vote = s_ote.next(1)
        P.on("vector", lambda e, s=s_ote.h: e.tensor_copy(
            o_fm[:, :], ps_tr[:, 0:512]).then_inc(s, 1))
        P.wait("tensor", s_ote, vote)
        vwof = s_wof.next(1)

        def _wo(e, s=s_wof.h):
            for nc2 in range(2):
                for c in range(4):
                    ins = e.matmul(ps_o[:, 512 + 0:512 + 0] if False else
                                   ps_o[:, nc2 * 512:(nc2 + 1) * 512],
                                   o_fm[:, c * 128:(c + 1) * 128],
                                   wo_sb[:, c * D + nc2 * 512:
                                         c * D + nc2 * 512 + 512],
                                   start=(c == 0), stop=(c == 3))
            ins.then_inc(s, 1)
        P.on("tensor", _wo)
        P.wait("vector", s_wof, vwof)
        vxu = s_xu.next(1)
        P.on("vector", lambda e, i=i, s=s_xu.h: e.tensor_add(
            ap(xi, i * D, [[1, D]]), ap(xi, i * D, [[1, D]]),
            ps_o[:, 0:D]).then_inc(s, 1))
        par = i % 2
        P.wait("sync", s_xu, vxu)
        s_st[par].next(16)
        P.on("sync", lambda e, i=i, s=s_st[par].h: e.dma_start(
            out=x_out.rearrange("(n p) d -> n p d", p=128)[i],
            in_=ap(xi, i * D, [[1, D]])).then_inc(s, 16))

    if dbg:
        dbg_v = P.dram_out("dbg_v", [128, NT * INNER8])
        dbg_bk = P.dram_out("dbg_bk", [64, HEADS * T])
        s_dbg = P.sem("s_dbg")
        P.wait("gpsimd", s_qke, s_qke.n)
        P.on("gpsimd", lambda e, s=s_dbg.h: e.dma_start(
            out=dbg_v[:, :], in_=v_tm[:, :]).then_inc(s, 16))
        P.wait("gpsimd", s_bqe, s_bqe.n)
        P.on("gpsimd", lambda e, s=s_dbg.h: e.dma_start(
            out=dbg_bk[:, :], in_=bkf[:, :]).then_inc(s, 16))
        P.wait("gpsimd", s_dbg, 32)
    P.wait("sync", s_st[0], s_st[0].n)
    P.wait("sync", s_st[1], s_st[1].n)
    return P


def emit_ln_inplace(P, src_fn, pref, eps=1e-5):
    stats_s = P.sbuf(pref + "_s", [128, NT], F32)
    stats_q = P.sbuf(pref + "_q", [128, NT], F32)
    rstd = P.sbuf(pref + "_rstd", [128, NT], F32)
    negmr = P.sbuf(pref + "_negmr", [128, NT], F32)
    scratch = P.sbuf(pref + "_scr", [128, NT], F32)
    junk, _ = P.ln_scratch
    s_stat = P.sem(pref + "_stat")
    s_sq = P.sem(pref + "_sq")
    s_prep = P.sem(pref + "_prep")
    s_sqrt = P.sem(pref + "_sqrt")
    s_norm = P.sem(pref + "_norm")
    for j in range(NT):
        xt = src_fn(j)
        s_stat.next(1)
        P.wait("scalar", s_sq, j)
        s_sq.next(1)
        P.on("scalar", lambda e, xt=xt, j=j, s=s_sq.h: e.activation(
            junk[:, :], xt, AF.Square,
            accum_out=stats_q[:, j:j + 1]).then_inc(s, 1))
        P.on("vector", lambda e, xt=xt, j=j, s=s_stat.h: e.reduce_sum(
            stats_s[:, j:j + 1], xt, axis=AX.X).then_inc(s, 1))
    P.wait("vector", s_stat, s_stat.n)
    P.wait("vector", s_sq, s_sq.n)
    v0 = s_prep.n
    s_prep.next(4)
    P.on("vector", lambda e, s=s_prep.h: e.tensor_scalar(
        stats_s[:, :], stats_s[:, :], 1.0 / D, None, ALU.mult).then_inc(s, 1))
    P.wait("vector", s_prep, v0 + 1)
    P.on("vector", lambda e, s=s_prep.h: e.tensor_scalar(
        stats_q[:, :], stats_q[:, :], 1.0 / D, float(eps),
        ALU.mult, ALU.add).then_inc(s, 1))
    P.wait("vector", s_prep, v0 + 2)
    P.on("vector", lambda e, s=s_prep.h: e.scalar_tensor_tensor(
        scratch[:, :], stats_s[:, :], 1.0, stats_s[:, :],
        ALU.mult, ALU.mult).then_inc(s, 1))
    P.wait("vector", s_prep, v0 + 3)
    P.on("vector", lambda e, s=s_prep.h: e.tensor_sub(
        stats_q[:, :], stats_q[:, :], scratch[:, :]).then_inc(s, 1))
    P.wait("scalar", s_prep, v0 + 4)
    v2 = s_sqrt.next(1)
    P.on("scalar", lambda e, s=s_sqrt.h: e.activation(
        rstd[:, :], stats_q[:, :], AF.Sqrt).then_inc(s, 1))
    P.wait("vector", s_sqrt, v2)
    v4 = s_prep.n
    s_prep.next(2)
    P.on("vector", lambda e, s=s_prep.h: e.reciprocal(
        rstd[:, :], rstd[:, :]).then_inc(s, 1))
    P.wait("vector", s_prep, v4 + 1)
    P.on("vector", lambda e, s=s_prep.h: e.scalar_tensor_tensor(
        negmr[:, :], stats_s[:, :], -1.0, rstd[:, :],
        ALU.mult, ALU.mult).then_inc(s, 1))
    P.wait("scalar", s_prep, v4 + 2)
    for j in range(NT):
        xt = src_fn(j)
        s_norm.next(1)
        P.on("scalar", lambda e, xt=xt, j=j, s=s_norm.h: e.activation(
            xt, xt, AF.Identity, bias=negmr[:, j:j + 1],
            scale=rstd[:, j:j + 1]).then_inc(s, 1))
    return s_norm



HID2 = 2730
OCW = [128] * 21 + [42]


def build_l1_layer_mlp(final_add, stage="full"):
    P = Prog("l1m")
    x_in = P.dram_in("x", [T, D])
    lg_d = P.dram_in("ln_g", [D])
    lb_d = P.dram_in("ln_b", [D])
    w1_d = P.dram_in("w1", [D, 2 * HID2])
    b1_d = P.dram_in("b1", [2 * HID2])
    w2_d = P.dram_in("w2", [HID2, D])
    b2_d = P.dram_in("b2", [D])
    if final_add:
        x2_d = P.dram_in("x2", [T, D])
    x_out = P.dram_out("xo", [T, D])

    cc = host_consts_common()
    ident_d = P.const_in("c_ident", cc["ident"])
    ones_d = P.const_in("c_ones", np.ones((1, 128), dtype=ml_dtypes.bfloat16))
    onesf_d = P.const_in("c_onesf", np.ones((1, 128), dtype=np.float32))

    P.ident_sb = P.sbuf("ident", [128, 128], BF16)
    ones_sb = P.sbuf("ones", [1, 128], BF16)
    onesf_sb = P.sbuf("onesf", [1, 128], F32)
    xi = P.sbuf("xi", [128, NT * D], F32)
    xs2 = P.sbuf("xs2", [128, NT * 512], BF16)
    w2b = P.sbuf("w2b", [128, 22 * D], BF16)
    w1a_1 = P.sbuf("w1a", [128, KC * 128], BF16)
    w1g_1 = P.sbuf("w1g", [128, KC * 128], BF16)
    w1a = [w1a_1, w1a_1]
    w1g = [w1g_1, w1g_1]
    act_fm = P.sbuf("act_fm", [128, 22 * 1024], BF16)
    hn_fm = P.sbuf("hn_fm", [128, KC * 1024], BF16)
    hn_tm = P.sbuf("hn_tm", [128, D], BF16)
    gel_sb = [P.sbuf("gel0", [128, 1024], BF16)]
    gel_sb.append(gel_sb[0])
    bcols = P.sbuf("bcols", [128, 44], F32)
    brow = P.sbuf("brow", [1, 1024], F32)
    lgb = P.sbuf("lgb", [128, D], BF16)
    lbb = P.sbuf("lbb", [128, D], BF16)
    lrow = brow
    b2row = P.sbuf("b2row", [1, D], BF16)
    st_s1 = P.sbuf("st_s1", [128, NT], F32)
    st_s2 = P.sbuf("st_s2", [128, NT], F32)
    st_q1 = P.sbuf("st_q1", [128, NT], F32)
    st_q2 = P.sbuf("st_q2", [128, NT], F32)
    rstd = P.sbuf("rstd", [128, NT], F32)
    negmr = P.sbuf("negmr", [128, NT], F32)
    scr = P.sbuf("scr", [128, NT], F32)
    junk = hn_tm
    xadd_1 = P.sbuf("xadd", [128, D], F32)
    xadd = [xadd_1, xadd_1]

    ps_a = P.psum("ps_a", [128, 1024], F32)
    ps_g = P.psum("ps_g", [128, 1024], F32)
    ps_y = P.psum("ps_y", [128, 1024], F32)
    ps_tr = P.psum("ps_tr", [128, 1024], BF16)

    s_x = P.sem("s_x")
    s_c = P.sem("s_c")
    s_cg = P.sem("s_cg")
    s_sh = P.sem("s_sh")
    s_st = [P.sem("st0"), P.sem("st1")]

    vx = s_x.next(16)
    P.on("sync", lambda e, s=s_x.h: e.dma_start(
        out=xi[:, :], in_=x_in.rearrange("(n p) d -> p n d", p=128)
    ).then_inc(s, 16))
    nc_ = 0
    for dst, src in ((P.ident_sb, ident_d), (ones_sb, ones_d),
                     (onesf_sb, onesf_d)):
        s_c.next(16)
        nc_ += 16
        P.on("sync", lambda e, dst=dst, src=src, s=s_c.h: e.dma_start(
            out=dst[:, :], in_=src[:, :]).then_inc(s, 16))
    vc = nc_
    lvl = {"L0": 0, "L1": 1, "L2": 2, "L3": 3}.get(stage, 99)
    s_w2 = P.sem("s_w2")
    vw2 = 0
    if lvl >= 1:
        P.on("gpsimd", lambda e, s=s_w2.h: e.dma_start(
            out=ap(w2b, 0, [[1024, 21], [1, 1024]]),
            in_=w2_d[0:2688, :].rearrange("(c p) n -> p c n", p=128)
        ).then_inc(s, 16))
        P.on("gpsimd", lambda e, s=s_w2.h: e.dma_start(
            out=bass.AP(w2b.ap().tensor, 21 * 1024,
                        [[w2b.ap().ap[0][0], 42], [1, 1024]]),
            in_=w2_d[2688:2730, :]).then_inc(s, 16))
        vw2 = s_w2.next(32)
    vcg = 0
    if lvl >= 2:
        P.on("gpsimd", lambda e, s=s_cg.h: e.dma_start(
            out=b2row[0:1, :], in_=b2_d).then_inc(s, 16))
        vcg = s_cg.next(16)

    sh2 = 0
    vshm = 0
    s_shm = P.sem("s_shm")
    if lvl >= 3:
        P.wait("gpsimd", s_x, vx)
        for t0 in (0, 8):
            s_sh.next(16)
            P.on("gpsimd", lambda e, t0=t0, s=s_sh.h: e.dma_start(
                out=bass.AP(xs2.ap().tensor,
                            xs2.ap().ap[0][0] + t0 * 512,
                            [[xs2.ap().ap[0][0], 127], [512, 8], [1, 512]]),
                in_=bass.AP(xi.ap().tensor, t0 * D + 512,
                            [[xi.ap().ap[0][0], 127], [D, 8], [1, 512]])
            ).then_inc(s, 16))
        sh2 = s_sh.next(16)
        P.on("gpsimd", lambda e, s=s_sh.h: e.dma_start(
            out=bass.AP(xs2.ap().tensor, 512,
                        [[xs2.ap().ap[0][0], 1], [512, NT - 1], [1, 512]]),
            in_=bass.AP(xi.ap().tensor, 127 * xi.ap().ap[0][0] + 512,
                        [[xi.ap().ap[0][0], 1], [D, NT - 1], [1, 512]])
        ).then_inc(s, 16))
    if lvl >= 4:
        P.wait("gpsimd", s_sh, sh2)
        vshm = s_shm.next(1)
        P.on("gpsimd", lambda e, s=s_shm.h: e.memset(
            bass.AP(xs2.ap().tensor, 0, [[xs2.ap().ap[0][0], 1], [1, 512]]),
            0.0).then_inc(s, 1))

    if stage in ("loads", "L0", "L1", "L2", "L3"):
        dbg = P.dram_out("dbg", [128, 1024])
        s_dbg = P.sem("s_dbg")
        P.wait("sync", s_x, vx)
        P.wait("sync", s_c, vc)
        if vw2:
            P.wait("sync", s_w2, vw2)
        if vcg:
            P.wait("sync", s_cg, vcg)
        if sh2:
            P.wait("sync", s_sh, sh2)
        if vshm:
            P.wait("sync", s_shm, vshm)
        P.on("sync", lambda e, s=s_dbg.h: e.dma_start(
            out=dbg[:, :], in_=xi[:, 0:1024]).then_inc(s, 16))
        P.wait("sync", s_dbg, 16)
        return P
    if stage == "shift":
        dbg = P.dram_out("dbg", [128, NT * 512], BF16)
        s_dbg = P.sem("s_dbg")
        P.wait("sync", s_x, vx)
        P.wait("sync", s_c, vc)
        P.wait("sync", s_w2, vw2)
        P.wait("sync", s_cg, vcg)
        P.wait("sync", s_sh, sh2)
        P.wait("sync", s_shm, vshm)
        P.on("sync", lambda e, s=s_dbg.h: e.dma_start(
            out=dbg[:, :], in_=xs2[:, :]).then_inc(s, 16))
        P.wait("sync", s_dbg, 16)
        return P

    s_lr = P.sem("s_lr")
    s_lb = P.sem("s_lb")
    v1 = s_lr.next(16)
    P.on("sync", lambda e, s=s_lr.h: e.dma_start(
        out=lrow[0:1, :], in_=lg_d).then_inc(s, 16))
    P.wait("tensor", s_lr, v1)
    P.wait("tensor", s_c, vc)
    vb1 = s_lb.next(1)

    def _lgb(e, s=s_lb.h):
        for nc2 in range(2):
            ins = e.matmul(ps_a[:, nc2 * 512:(nc2 + 1) * 512],
                           onesf_sb[0:1, :], lrow[0:1, nc2 * 512:
                                                   (nc2 + 1) * 512],
                           start=True, stop=True)
        ins.then_inc(s, 1)
    P.on("tensor", _lgb)
    P.wait("vector", s_lb, vb1)
    vbe1 = s_lb.next(1)
    P.on("vector", lambda e, s=s_lb.h: e.tensor_copy(
        lgb[:, :], ps_a[:, 0:D]).then_inc(s, 1))
    P.wait("sync", s_lb, vbe1)
    v2 = s_lr.next(16)
    P.on("sync", lambda e, s=s_lr.h: e.dma_start(
        out=lrow[0:1, :], in_=lb_d).then_inc(s, 16))
    P.wait("tensor", s_lr, v2)
    vb2 = s_lb.next(1)

    def _lbb(e, s=s_lb.h):
        for nc2 in range(2):
            ins = e.matmul(ps_a[:, nc2 * 512:(nc2 + 1) * 512],
                           onesf_sb[0:1, :], lrow[0:1, nc2 * 512:
                                                   (nc2 + 1) * 512],
                           start=True, stop=True)
        ins.then_inc(s, 1)
    P.on("tensor", _lbb)
    P.wait("vector", s_lb, vb2)
    vbe2 = s_lb.next(1)
    P.on("vector", lambda e, s=s_lb.h: e.tensor_copy(
        lbb[:, :], ps_a[:, 0:D]).then_inc(s, 1))

    if stage == "bcast":
        dbg = P.dram_out("dbg", [128, 2 * D], BF16)
        s_dbg = P.sem("s_dbg")
        P.wait("sync", s_x, vx)
        P.wait("sync", s_w2, vw2)
        P.wait("sync", s_cg, vcg)
        P.wait("sync", s_sh, sh2)
        P.wait("sync", s_shm, vshm)
        P.wait("sync", s_lb, vbe2)
        P.on("sync", lambda e, s=s_dbg.h: e.dma_start(
            out=dbg[:, 0:D], in_=lgb[:, :]).then_inc(s, 16))
        P.on("sync", lambda e, s=s_dbg.h: e.dma_start(
            out=dbg[:, D:2 * D], in_=lbb[:, :]).then_inc(s, 16))
        P.wait("sync", s_dbg, 32)
        return P

    s_br = P.sem("s_br")
    s_bt = P.sem("s_bt")
    s_be = P.sem("s_be")
    pieces = []
    for part in range(2):
        base_off = part * HID2
        got = 0
        while got < HID2:
            w = min(1024, HID2 - got)
            pieces.append((base_off + got, got, part, w))
            got += w
    nj = 0
    P.wait("sync", s_lb, vb2)
    for (srcoff, dstoff, part, w) in pieces:
        if nj >= 1:
            P.wait("sync", s_bt, nj)
        vbr = s_br.next(16)
        P.on("sync", lambda e, srcoff=srcoff, w=w, s=s_br.h: e.dma_start(
            out=brow[0:1, 0:w], in_=b1_d[srcoff:srcoff + w]).then_inc(s, 16))
        P.wait("tensor", s_br, vbr)
        if nj >= 1:
            P.wait("tensor", s_be, nj)
        vbt = s_bt.next(1)

        def _bt(e, dstoff=dstoff, part=part, w=w, s=s_bt.h):
            done = 0
            while done < w:
                cw_ = min(128, w - done)
                col = (dstoff + done) // 128
                sub = (dstoff + done) % 128
                ins = e.transpose(
                    bass.AP(ps_g.ap().tensor, 16 + col,
                            [[ps_g.ap().ap[0][0], cw_], [1, 1]])
                    if False else
                    bass.AP(ps_g.ap().tensor, done // 128 + 16,
                            [[ps_g.ap().ap[0][0], cw_], [1, 1]]),
                    brow[0:1, done:done + cw_],
                    bass.AP(onesf_sb.ap().tensor, 0, [[128, 1], [1, 1]]))
                done += cw_
            ins.then_inc(s, 1)
        P.on("tensor", _bt)
        P.wait("vector", s_bt, vbt)
        s_be.next(1)
        ncols = (w + 127) // 128
        P.on("vector", lambda e, dstoff=dstoff, part=part, ncols=ncols,
             s=s_be.h: e.tensor_copy(
                 bcols[:, part * 22 + dstoff // 128:
                       part * 22 + dstoff // 128 + ncols],
                 ps_g[:, 16:16 + ncols]).then_inc(s, 1))
        nj += 1
    nbias = s_be.n

    if stage == "bias":
        dbg = P.dram_out("dbg", [128, 44])
        s_dbg = P.sem("s_dbg")
        P.wait("sync", s_x, vx)
        P.wait("sync", s_w2, vw2)
        P.wait("sync", s_cg, vcg)
        P.wait("sync", s_sh, sh2)
        P.wait("sync", s_shm, vshm)
        P.wait("sync", s_be, nbias)
        P.on("sync", lambda e, s=s_dbg.h: e.dma_start(
            out=dbg[:, :], in_=bcols[:, :]).then_inc(s, 16))
        P.wait("sync", s_dbg, 16)
        return P

    s_stat = P.sem("s_stat")
    s_sq = P.sem("s_sq")
    s_prep = P.sem("s_prep")
    s_sqrt = P.sem("s_sqrt")
    P.wait("scalar", s_x, vx)
    P.wait("vector", s_x, vx)
    P.wait("scalar", s_sh, sh2)
    P.wait("vector", s_sh, sh2)
    P.wait("scalar", s_shm, vshm)
    P.wait("vector", s_shm, vshm)
    for j in range(NT):
        x1 = ap(xi, j * D, [[1, 512]])
        x2 = ap(xs2, j * 512, [[1, 512]])
        s_stat.next(2)
        P.wait("scalar", s_sq, 2 * j)
        s_sq.next(2)
        P.on("scalar", lambda e, x1=x1, j=j, s=s_sq.h: e.activation(
            junk[:, 0:512], x1, AF.Square,
            accum_out=st_q1[:, j:j + 1]).then_inc(s, 1))
        P.on("scalar", lambda e, x2=x2, j=j, s=s_sq.h: e.activation(
            junk[:, 0:512], x2, AF.Square,
            accum_out=st_q2[:, j:j + 1]).then_inc(s, 1))
        P.on("vector", lambda e, x1=x1, j=j, s=s_stat.h: e.reduce_sum(
            st_s1[:, j:j + 1], x1, axis=AX.X).then_inc(s, 1))
        P.on("vector", lambda e, x2=x2, j=j, s=s_stat.h: e.reduce_sum(
            st_s2[:, j:j + 1], x2, axis=AX.X).then_inc(s, 1))
    P.wait("vector", s_stat, s_stat.n)
    P.wait("vector", s_sq, s_sq.n)
    p0 = s_prep.n
    s_prep.next(6)
    P.on("vector", lambda e, s=s_prep.h: e.tensor_add(
        st_s1[:, :], st_s1[:, :], st_s2[:, :]).then_inc(s, 1))
    P.wait("vector", s_prep, p0 + 1)
    P.on("vector", lambda e, s=s_prep.h: e.tensor_add(
        st_q1[:, :], st_q1[:, :], st_q2[:, :]).then_inc(s, 1))
    P.wait("vector", s_prep, p0 + 2)
    P.on("vector", lambda e, s=s_prep.h: e.tensor_scalar(
        st_s1[:, :], st_s1[:, :], 1.0 / D, None, ALU.mult).then_inc(s, 1))
    P.wait("vector", s_prep, p0 + 3)
    P.on("vector", lambda e, s=s_prep.h: e.tensor_scalar(
        st_q1[:, :], st_q1[:, :], 1.0 / D, 1e-5,
        ALU.mult, ALU.add).then_inc(s, 1))
    P.wait("vector", s_prep, p0 + 4)
    P.on("vector", lambda e, s=s_prep.h: e.scalar_tensor_tensor(
        scr[:, :], st_s1[:, :], 1.0, st_s1[:, :],
        ALU.mult, ALU.mult).then_inc(s, 1))
    P.wait("vector", s_prep, p0 + 5)
    P.on("vector", lambda e, s=s_prep.h: e.tensor_sub(
        st_q1[:, :], st_q1[:, :], scr[:, :]).then_inc(s, 1))
    P.wait("scalar", s_prep, p0 + 6)
    vsq2 = s_sqrt.next(1)
    P.on("scalar", lambda e, s=s_sqrt.h: e.activation(
        rstd[:, :], st_q1[:, :], AF.Sqrt).then_inc(s, 1))
    P.wait("vector", s_sqrt, vsq2)
    p6 = s_prep.n
    s_prep.next(2)
    P.on("vector", lambda e, s=s_prep.h: e.reciprocal(
        rstd[:, :], rstd[:, :]).then_inc(s, 1))
    P.wait("vector", s_prep, p6 + 1)
    P.on("vector", lambda e, s=s_prep.h: e.scalar_tensor_tensor(
        negmr[:, :], st_s1[:, :], -1.0, rstd[:, :],
        ALU.mult, ALU.mult).then_inc(s, 1))
    v_stats = s_prep.n
    P.wait("scalar", s_prep, v_stats)
    P.wait("vector", s_lb, vbe2)
    if stage == "stats":
        dbg = P.dram_out("dbg", [128, NT])
        s_dbg = P.sem("s_dbg")
        P.wait("sync", s_prep, v_stats)
        P.wait("sync", s_be, nbias)
        P.wait("sync", s_w2, vw2)
        P.on("sync", lambda e, s=s_dbg.h: e.dma_start(
            out=dbg[:, :], in_=rstd[:, :]).then_inc(s, 16))
        P.wait("sync", s_dbg, 16)
        return P

    s_hna = P.sem("s_hna")
    s_hnl = P.sem("s_hnl")
    s_htr = P.sem("s_htr")
    s_hte = P.sem("s_hte")
    s_w1d = P.sem("s_w1d")
    s_af = P.sem("s_af")
    s_gf = P.sem("s_gf")
    s_ge = P.sem("s_ge")
    s_ac = P.sem("s_ac")
    s_yf = P.sem("s_yf")
    s_xu = P.sem("s_xu")
    s_xa = [P.sem("s_xa0"), P.sem("s_xa1")]
    P.wait("tensor", s_be, nbias)
    P.wait("tensor", s_w2, vw2)
    P.wait("tensor", s_cg, vcg)

    for tb in range(2):
        for oc in range(22):
            slot = oc % 2
            ocw = OCW[oc]
            na = 22 * tb + oc
            if na >= 1:
                P.wait("gpsimd", s_af, na)
                P.wait("gpsimd", s_gf, na)
            s_w1d.next(32)
            P.on("gpsimd", lambda e, oc=oc, ocw=ocw, slot=slot,
                 s=s_w1d.h: e.dma_start(
                     out=ap(w1a[slot], 0, [[128, KC], [1, ocw]]),
                     in_=w1_d[:, oc * 128:oc * 128 + ocw].rearrange(
                         "(c p) n -> p c n", p=128)).then_inc(s, 16))
            P.on("gpsimd", lambda e, oc=oc, ocw=ocw, slot=slot,
                 s=s_w1d.h: e.dma_start(
                     out=ap(w1g[slot], 0, [[128, KC], [1, ocw]]),
                     in_=w1_d[:, HID2 + oc * 128:HID2 + oc * 128 + ocw
                              ].rearrange("(c p) n -> p c n", p=128)
                 ).then_inc(s, 16))

    for tb in range(2):
        for jj in range(8):
            j = tb * 8 + jj
            vha = s_hna.next(2)
            if s_htr.n >= 1:
                P.wait("scalar", s_htr, s_htr.n)
            P.on("scalar", lambda e, j=j, s=s_hna.h: e.activation(
                hn_tm[:, 0:512], ap(xi, j * D, [[1, 512]]), AF.Identity,
                bias=negmr[:, j:j + 1], scale=rstd[:, j:j + 1]
            ).then_inc(s, 1))
            P.on("scalar", lambda e, j=j, s=s_hna.h: e.activation(
                hn_tm[:, 512:1024], ap(xs2, j * 512, [[1, 512]]),
                AF.Identity, bias=negmr[:, j:j + 1],
                scale=rstd[:, j:j + 1]).then_inc(s, 1))
            P.wait("vector", s_hna, vha)
            vhl = s_hnl.next(2)
            P.on("vector", lambda e, s=s_hnl.h: e.tensor_mul(
                hn_tm[:, :], hn_tm[:, :], lgb[:, :]).then_inc(s, 1))
            P.wait("vector", s_hnl, vhl - 1)
            P.on("vector", lambda e, s=s_hnl.h: e.tensor_add(
                hn_tm[:, :], hn_tm[:, :], lbb[:, :]).then_inc(s, 1))
            P.wait("tensor", s_hnl, vhl)
            if jj >= 2 or (tb > 0 and jj < 2):
                P.wait("tensor", s_hte, s_hte.n)
            vht = s_htr.next(1)

            def _htr(e, s=s_htr.h):
                for c in range(KC):
                    ins = e.transpose(ps_tr[:, c * 128:(c + 1) * 128],
                                      hn_tm[:, c * 128:(c + 1) * 128],
                                      P.ident_sb[:, :])
                ins.then_inc(s, 1)
            P.on("tensor", _htr)
            P.wait("vector", s_htr, vht)
            if jj == 0 and tb > 0:
                P.wait("vector", s_gf, s_gf.n)
            s_hte.next(1)
            out_ap = ap(hn_fm, jj * 128, [[1024, KC], [1, 128]])
            P.on("vector", lambda e, o=out_ap, s=s_hte.h: e.tensor_copy(
                o, ps_tr[:, :]).then_inc(s, 1))
        if stage == "hn":
            continue
        P.wait("tensor", s_hte, s_hte.n)
        for oc in range(22):
            slot = oc % 2
            ocw = OCW[oc]
            na_g = 22 * tb + oc
            P.wait("tensor", s_w1d, 32 * (na_g + 1))
            vaf = s_af.next(1)
            if s_ge.n >= 1:
                P.wait("tensor", s_ge, s_ge.n)

            def _af(e, slot=slot, ocw=ocw, s=s_af.h):
                for nc2 in range(2):
                    for kc in range(KC):
                        ins = e.matmul(
                            bass.AP(ps_a.ap().tensor, nc2 * 512,
                                    [[ps_a.ap().ap[0][0], ocw], [1, 512]]),
                            ap(w1a[slot], kc * 128, [[1, ocw]]),
                            ap(hn_fm, kc * 1024 + nc2 * 512, [[1, 512]]),
                            start=(kc == 0), stop=(kc == KC - 1))
                ins.then_inc(s, 1)
            P.on("tensor", _af)
            vgf = s_gf.next(1)
            if s_ac.n >= 1:
                P.wait("tensor", s_ac, s_ac.n)

            def _gf(e, slot=slot, ocw=ocw, s=s_gf.h):
                for nc2 in range(2):
                    for kc in range(KC):
                        ins = e.matmul(
                            bass.AP(ps_g.ap().tensor, nc2 * 512,
                                    [[ps_g.ap().ap[0][0], ocw], [1, 512]]),
                            ap(w1g[slot], kc * 128, [[1, ocw]]),
                            ap(hn_fm, kc * 1024 + nc2 * 512, [[1, 512]]),
                            start=(kc == 0), stop=(kc == KC - 1))
                ins.then_inc(s, 1)
            P.on("tensor", _gf)
            P.wait("scalar", s_gf, vgf)
            if s_ac.n >= 1:
                P.wait("scalar", s_ac, s_ac.n)
            vge = s_ge.next(1)
            gbuf = gel_sb[oc % 2]
            P.on("scalar", lambda e, oc=oc, ocw=ocw, gbuf=gbuf,
                 s=s_ge.h: e.activation(
                     bass.AP(gbuf.ap().tensor, 0,
                             [[gbuf.ap().ap[0][0], ocw], [1, 1024]]),
                     bass.AP(ps_g.ap().tensor, 0,
                             [[ps_g.ap().ap[0][0], ocw], [1, 1024]]),
                     AF.Gelu,
                     bias=bass.AP(bcols.ap().tensor, 22 + oc,
                                  [[bcols.ap().ap[0][0], ocw], [1, 1]])
                 ).then_inc(s, 1))
            P.wait("vector", s_ge, vge)
            P.wait("vector", s_af, vaf)
            s_ac.next(1)
            P.on("vector", lambda e, oc=oc, ocw=ocw, gbuf=gbuf,
                 s=s_ac.h: e.scalar_tensor_tensor(
                     bass.AP(act_fm.ap().tensor, oc * 1024,
                             [[act_fm.ap().ap[0][0], ocw], [1, 1024]]),
                     bass.AP(ps_a.ap().tensor, 0,
                             [[ps_a.ap().ap[0][0], ocw], [1, 1024]]),
                     bass.AP(bcols.ap().tensor, oc,
                             [[bcols.ap().ap[0][0], ocw], [1, 1]]),
                     bass.AP(gbuf.ap().tensor, 0,
                             [[gbuf.ap().ap[0][0], ocw], [1, 1024]]),
                     ALU.add, ALU.mult).then_inc(s, 1))
        if stage == "w1":
            continue
        P.wait("tensor", s_ac, s_ac.n)
        for jj in range(8):
            j = tb * 8 + jj
            vyf = s_yf.next(1)
            if s_xu.n >= 1:
                P.wait("tensor", s_xu, s_xu.n)

            def _yf(e, jj=jj, s=s_yf.h):
                for nc2 in range(2):
                    for oc in range(22):
                        ocw = OCW[oc]
                        e.matmul(ps_y[:, nc2 * 512:(nc2 + 1) * 512],
                                 bass.AP(act_fm.ap().tensor,
                                         oc * 1024 + jj * 128,
                                         [[act_fm.ap().ap[0][0], ocw],
                                          [1, 128]]),
                                 bass.AP(w2b.ap().tensor,
                                         oc * D + nc2 * 512,
                                         [[w2b.ap().ap[0][0], ocw],
                                          [1, 512]]),
                                 start=(oc == 0), stop=False)
                    ins = e.matmul(ps_y[:, nc2 * 512:(nc2 + 1) * 512],
                                   ones_sb[0:1, :],
                                   b2row[0:1, nc2 * 512:(nc2 + 1) * 512],
                                   start=False, stop=True)
                ins.then_inc(s, 1)
            P.on("tensor", _yf)
            P.wait("vector", s_yf, vyf)
            vxu = s_xu.next(1)
            P.on("vector", lambda e, j=j, s=s_xu.h: e.tensor_add(
                ap(xi, j * D, [[1, D]]), ap(xi, j * D, [[1, D]]),
                ps_y[:, :]).then_inc(s, 1))
            par = j % 2
            if final_add:
                if j >= 1:
                    P.wait("sync", s_st[(j - 1) % 2],
                           16 * ((j - 1) // 2 + 1))
                vxa = s_xa[par].next(16)
                P.on("sync", lambda e, j=j, s=s_xa[par].h: e.dma_start(
                    out=xadd[par][:, :],
                    in_=x2_d.rearrange("(n p) d -> n p d", p=128)[j]
                ).then_inc(s, 16))
                P.wait("vector", s_xa[par], vxa)
                P.wait("vector", s_xu, vxu)
                vxu = s_xu.next(1)
                P.on("vector", lambda e, j=j, par=par,
                     s=s_xu.h: e.scalar_tensor_tensor(
                         xadd[par][:, :], ap(xi, j * D, [[1, D]]), 0.5,
                         xadd[par][:, :], ALU.mult, ALU.add).then_inc(s, 1))
                P.wait("sync", s_xu, vxu)
                s_st[par].next(16)
                P.on("sync", lambda e, j=j, par=par,
                     s=s_st[par].h: e.dma_start(
                         out=x_out.rearrange("(n p) d -> n p d", p=128)[j],
                         in_=xadd[par][:, :]).then_inc(s, 16))
            else:
                P.wait("sync", s_xu, vxu)
                s_st[par].next(16)
                P.on("sync", lambda e, j=j, par=par,
                     s=s_st[par].h: e.dma_start(
                         out=x_out.rearrange("(n p) d -> n p d", p=128)[j],
                         in_=ap(xi, j * D, [[1, D]])).then_inc(s, 16))

    P.wait("sync", s_st[0], s_st[0].n)
    P.wait("sync", s_st[1], s_st[1].n)
    return P




NH = 32
HD = 64
DS = 128
DI = 2048
CONVD = DI + 2 * DS
NCB = 18
DIP = 2 * DI + 2 * DS + NH


def build_m1(dbg=False):
    P = Prog("m1")
    x_in = P.dram_in("x", [T, D])
    w_d = P.dram_in("in_proj_w", [D, DIP])
    cw_d = P.dram_in("conv_weight", [CONVD, 1, 4])
    cb_d = P.dram_in("conv_bias", [CONVD])
    xbc_o = P.dram_out("xbc", [128, NCB * T], BF16)
    z_o = P.dram_out("z", [128, NT * DI], BF16)
    dt_o = P.dram_out("dt", [128, NT * NH])

    cc = host_consts_common()
    ident_d = P.const_in("c_ident", cc["ident"])
    onesf_d = P.const_in("c_onesf", np.ones((1, 128), dtype=np.float32))

    P.ident_sb = P.sbuf("ident", [128, 128], BF16)
    onesf_sb = P.sbuf("onesf", [1, 128], F32)
    P.ln_scratch = (P.sbuf("lnjunk", [128, D], BF16),
                    [P.sbuf(f"lntm{j}", [128, D], BF16) for j in range(2)])
    x_sb = P.sbuf("x_sb", [128, NT * D], F32)
    xn_fm = P.sbuf("xn_fm", [128, KC * T], BF16)
    CSTR = T + 4
    xbc_fm = P.sbuf("xbc_fm", [128, NCB * CSTR], BF16)
    wz = [P.sbuf(f"wz{j}", [128, KC * 512], BF16) for j in range(2)]
    zblk = [P.sbuf(f"zblk{j}", [128, 512], BF16) for j in range(2)]
    dt_tm = P.sbuf("dt_tm", [128, NT * NH], F32)
    cw_sb = P.sbuf("cw_sb", [128, NCB * 4], F32)
    cb_sb = P.sbuf("cb_sb", [128, NCB], F32)
    brow = P.sbuf("brow", [1, 768], F32)
    conv_acc = P.sbuf("conv_acc", [128, 1024], F32)
    xbcc = P.sbuf("xbcc", [128, 1024], BF16)

    ps_z = P.psum("ps_z", [128, 512], F32)
    ps_x = P.psum("ps_x", [128, 512], F32)
    ps_trA = P.psum("ps_trA", [128, 1024], BF16)
    ps_trB = P.psum("ps_trB", [128, 1024], BF16)
    ps_b = P.psum("ps_b", [128, 512], F32)

    s_x = P.sem("s_x")
    s_c = P.sem("s_c")
    s_cg = P.sem("s_cg")
    s_wd = P.sem("s_wd")
    s_st = P.sem("s_st")
    s_stz = [P.sem("s_stz0"), P.sem("s_stz1")]

    vx = s_x.next(16)
    P.on("sync", lambda e, s=s_x.h: e.dma_start(
        out=x_sb[:, :], in_=x_in.rearrange("(n p) d -> p n d", p=128)
    ).then_inc(s, 16))
    vcn0 = s_c.next(16)
    P.on("sync", lambda e, s=s_c.h: e.dma_start(
        out=P.ident_sb[:, :], in_=ident_d[:, :]).then_inc(s, 16))
    vcn = s_c.next(16)
    P.on("sync", lambda e, s=s_c.h: e.dma_start(
        out=onesf_sb[:, :], in_=onesf_d[:, :]).then_inc(s, 16))
    vcg1 = s_cg.next(16)
    P.on("gpsimd", lambda e, s=s_cg.h: e.dma_start(
        out=cw_sb[:, :],
        in_=cw_d.rearrange("(c p) o k -> p c (o k)", p=128)).then_inc(s, 16))

    s_br = P.sem("s_br")
    s_bt = P.sem("s_bt")
    s_be = P.sem("s_be")
    P.wait("tensor", s_c, vcn)
    for pc in range(3):
        if pc >= 1:
            P.wait("sync", s_bt, pc)
        vbr = s_br.next(16)
        P.on("sync", lambda e, pc=pc, s=s_br.h: e.dma_start(
            out=brow[0:1, :],
            in_=cb_d[pc * 768:(pc + 1) * 768]).then_inc(s, 16))
        P.wait("tensor", s_br, vbr)
        if pc >= 1:
            P.wait("tensor", s_be, pc)
        vbt = s_bt.next(1)

        def _bt(e, pc=pc, s=s_bt.h):
            for k in range(6):
                ins = e.transpose(
                    bass.AP(ps_b.ap().tensor, k,
                            [[ps_b.ap().ap[0][0], 128], [1, 1]]),
                    brow[0:1, k * 128:(k + 1) * 128],
                    bass.AP(onesf_sb.ap().tensor, 0, [[128, 1], [1, 1]]))
            ins.then_inc(s, 1)
        P.on("tensor", _bt)
        P.wait("vector", s_bt, vbt)
        s_be.next(1)
        P.on("vector", lambda e, pc=pc, s=s_be.h: e.tensor_copy(
            cb_sb[:, pc * 6:(pc + 1) * 6], ps_b[:, 0:6]).then_inc(s, 1))
    nbias = s_be.n

    P.wait("scalar", s_x, vx)
    P.wait("vector", s_x, vx)

    def srcx(i):
        return ap(x_sb, i * D, [[1, D]])
    s_fm, s_norm = emit_ln_to_fm(P, srcx, list(range(NT)), xn_fm,
                                 ps_trA, ps_trB, T, 0, "ln")
    P.wait("tensor", s_fm, NT)

    s_zf = P.sem("s_zf")
    s_ze = P.sem("s_ze")
    s_xf = P.sem("s_xf")
    s_xe = P.sem("s_xe")
    for nb in range(4):
        slot = nb % 2
        if nb >= 2:
            P.wait("gpsimd", s_zf, 16 * (nb - 1))
        s_wd.next(16)
        P.on("gpsimd", lambda e, nb=nb, slot=slot, s=s_wd.h: e.dma_start(
            out=wz[slot][:, :],
            in_=w_d[:, nb * 512:(nb + 1) * 512].rearrange(
                "(c p) n -> p c n", p=128)).then_inc(s, 16))
    for nb in range(4):
        slot = nb % 2
        P.wait("tensor", s_wd, 16 * (nb + 1))
        for i in range(NT):
            vzf = s_zf.next(1)
            if s_ze.n >= 1:
                P.wait("tensor", s_ze, s_ze.n)

            def _zf(e, i=i, slot=slot, s=s_zf.h):
                for kc in range(KC):
                    ins = e.matmul(ps_z[:, :],
                                   ap(xn_fm, kc * T + i * 128, [[1, 128]]),
                                   ap(wz[slot], kc * 512, [[1, 512]]),
                                   start=(kc == 0), stop=(kc == KC - 1))
                ins.then_inc(s, 1)
            P.on("tensor", _zf)
            P.wait("scalar", s_zf, vzf)
            slot2 = i % 2
            nuse = nb * NT + i
            if nuse >= 2:
                P.wait("scalar", s_stz[slot2], 16 * (nuse // 2))
            vze = s_ze.next(1)
            P.on("scalar", lambda e, slot2=slot2, s=s_ze.h: e.activation(
                zblk[slot2][:, :], ps_z[:, :], AF.Copy).then_inc(s, 1))
            P.wait("sync", s_ze, vze)
            s_stz[slot2].next(16)
            P.on("sync", lambda e, nb=nb, i=i, slot2=slot2,
                 s=s_stz[slot2].h: e.dma_start(
                     out=z_o[:, i * DI + nb * 512:i * DI + nb * 512 + 512],
                     in_=zblk[slot2][:, :]).then_inc(s, 16))
    s_pad = P.sem("s_pad")
    vpad = s_pad.next(2)
    P.on("gpsimd", lambda e, s=s_pad.h: e.memset(
        ap(xbc_fm, 0, [[CSTR, NCB], [1, 3]]), 0.0).then_inc(s, 1))
    P.on("gpsimd", lambda e, s=s_pad.h: e.memset(
        ap(xbc_fm, CSTR - 1, [[CSTR, NCB], [1, 1]]), 0.0).then_inc(s, 1))
    P.wait("vector", s_pad, vpad)
    for oc in range(NCB):
        slot = oc % 2
        nw = 4 + oc
        if oc < 2:
            P.wait("gpsimd", s_zf, 64)
        else:
            P.wait("gpsimd", s_xf, 4 * (oc - 1))
        s_wd.next(16)
        P.on("gpsimd", lambda e, oc=oc, slot=slot, s=s_wd.h: e.dma_start(
            out=ap(wz[slot], 0, [[512, KC], [1, 128]]),
            in_=w_d[:, DI + oc * 128:DI + oc * 128 + 128].rearrange(
                "(c p) n -> p c n", p=128)).then_inc(s, 16))
    for oc in range(NCB):
        slot = oc % 2
        nw = 4 + oc
        P.wait("tensor", s_wd, 16 * (nw + 1))
        for tb in range(4):
            vxf = s_xf.next(1)
            if s_xe.n >= 1:
                P.wait("tensor", s_xe, s_xe.n)

            def _xf(e, oc=oc, tb=tb, slot=slot, s=s_xf.h):
                for kc in range(KC):
                    ins = e.matmul(ps_x[:, :],
                                   ap(wz[slot], kc * 512, [[1, 128]]),
                                   ap(xn_fm, kc * T + tb * 512, [[1, 512]]),
                                   start=(kc == 0), stop=(kc == KC - 1))
                ins.then_inc(s, 1)
            P.on("tensor", _xf)
            P.wait("vector", s_xf, vxf)
            s_xe.next(1)
            P.on("vector", lambda e, oc=oc, tb=tb, s=s_xe.h: e.tensor_copy(
                ap(xbc_fm, oc * CSTR + 3 + tb * 512, [[1, 512]]),
                ps_x[:, :]).then_inc(s, 1))
    s_df = P.sem("s_df")
    s_de = P.sem("s_de")
    s_wd.next(16)
    P.wait("gpsimd", s_xf, s_xf.n)
    P.on("gpsimd", lambda e, s=s_wd.h: e.dma_start(
        out=ap(wz[0], 0, [[32, KC], [1, 32]]),
        in_=w_d[:, DIP - NH:DIP].rearrange("(c p) n -> p c n", p=128)
    ).then_inc(s, 16))
    P.wait("tensor", s_wd, s_wd.n)
    for i in range(NT):
        vdf = s_df.next(1)
        if s_de.n >= 1:
            P.wait("tensor", s_de, s_de.n)

        def _df(e, i=i, s=s_df.h):
            for kc in range(KC):
                ins = e.matmul(ps_z[:, 0:32],
                               ap(xn_fm, kc * T + i * 128, [[1, 128]]),
                               ap(wz[0], kc * 32, [[1, 32]]),
                               start=(kc == 0), stop=(kc == KC - 1))
            ins.then_inc(s, 1)
        P.on("tensor", _df)
        P.wait("vector", s_df, vdf)
        s_de.next(1)
        P.on("vector", lambda e, i=i, s=s_de.h: e.tensor_copy(
            dt_tm[:, i * NH:(i + 1) * NH], ps_z[:, 0:32]).then_inc(s, 1))
    P.wait("sync", s_de, s_de.n)
    vstd = s_st.next(16)
    P.on("sync", lambda e, s=s_st.h: e.dma_start(
        out=dt_o[:, :], in_=dt_tm[:, :]).then_inc(s, 16))

    if dbg:
        dbg_xf = P.dram_out("dbg_xf", [128, NCB * 2052], BF16)
        dbg_cw = P.dram_out("dbg_cw", [128, NCB * 4 + NCB])
        s_dbg = P.sem("s_dbg")
        P.wait("gpsimd", s_xe, s_xe.n)
        P.on("gpsimd", lambda e, s=s_dbg.h: e.dma_start(
            out=dbg_xf[:, :], in_=xbc_fm[:, :]).then_inc(s, 16))
        P.wait("gpsimd", s_be, nbias)
        P.wait("gpsimd", s_cg, vcg1)
        P.on("gpsimd", lambda e, s=s_dbg.h: e.dma_start(
            out=dbg_cw[:, 0:NCB * 4], in_=cw_sb[:, :]).then_inc(s, 16))
        P.on("gpsimd", lambda e, s=s_dbg.h: e.dma_start(
            out=dbg_cw[:, NCB * 4:], in_=cb_sb[:, :]).then_inc(s, 16))
        P.wait("gpsimd", s_dbg, 48)

    s_cv = P.sem("s_cv")
    s_sl = P.sem("s_sl")
    s_stx = P.sem("s_stx")
    P.wait("vector", s_xe, s_xe.n)
    P.wait("vector", s_cg, vcg1)
    P.wait("vector", s_be, nbias)
    P.wait("scalar", s_be, nbias)
    nhb = 0
    for oc in range(NCB):
        for hb in range(2):
            c0 = s_cv.n
            s_cv.next(4)
            if nhb >= 1:
                P.wait("vector", s_sl, nhb)
            base = oc * CSTR + hb * 1024
            P.on("vector", lambda e, base=base, oc=oc, s=s_cv.h:
                 e.tensor_scalar(
                     conv_acc[:, :], ap(xbc_fm, base + 0, [[1, 1024]]),
                     cw_sb[:, oc * 4:oc * 4 + 1], None,
                     ALU.mult).then_inc(s, 1))
            for k in range(1, 4):
                P.wait("vector", s_cv, c0 + k)
                P.on("vector", lambda e, base=base, oc=oc, k=k, s=s_cv.h:
                     e.scalar_tensor_tensor(
                         conv_acc[:, :], ap(xbc_fm, base + k, [[1, 1024]]),
                         cw_sb[:, oc * 4 + k:oc * 4 + k + 1], conv_acc[:, :],
                         ALU.mult, ALU.add).then_inc(s, 1))
            P.wait("scalar", s_cv, c0 + 4)
            if nhb >= 2:
                P.wait("scalar", s_stx, 16 * (nhb - 1))
            vsl = s_sl.next(1)
            P.on("scalar", lambda e, oc=oc, s=s_sl.h: e.activation(
                xbcc[:, :], conv_acc[:, :], AF.Silu,
                bias=cb_sb[:, oc:oc + 1]).then_inc(s, 1))
            P.wait("sync", s_sl, vsl)
            s_stx.next(16)
            P.on("sync", lambda e, oc=oc, hb=hb, s=s_stx.h: e.dma_start(
                out=xbc_o[:, oc * T + hb * 1024:oc * T + hb * 1024 + 1024],
                in_=xbcc[:, :]).then_inc(s, 16))
            nhb += 1
    P.wait("sync", s_st, s_st.n)
    P.wait("sync", s_stz[0], s_stz[0].n)
    P.wait("sync", s_stz[1], s_stz[1].n)
    P.wait("sync", s_stx, s_stx.n)
    return P



def build_m2(dbg=False):
    P = Prog("m2")
    x_in = P.dram_in("x", [T, D])
    xbc_i = P.dram_in("xbc", [128, NCB * T], BF16)
    z_i = P.dram_in("z", [128, NT * DI], BF16)
    dt_i = P.dram_in("dt", [128, NT * NH])
    dtb_d = P.dram_in("dt_bias", [NH])
    alog_d = P.dram_in("A_log", [NH])
    dpar_d = P.dram_in("D_param", [NH])
    nw_d = P.dram_in("norm_weight", [DI])
    ow_d = P.dram_in("out_proj_w", [DI, D])
    x_out = P.dram_out("xo", [T, D])

    cc = host_consts_common()
    ident_d = P.const_in("c_ident", cc["ident"])
    identf_d = P.const_in("c_identf", np.eye(128, dtype=np.float32))
    maskl_d = P.const_in("c_maskl", cc["mask_lead"])
    onesf_d = P.const_in("c_onesf", np.ones((1, 128), dtype=np.float32))
    ucs = np.triu(np.ones((128, 128), dtype=np.float32))
    ucs_d = P.const_in("c_ucs", ucs)
    nucs_d = P.const_in("c_nucs", -ucs)
    rs = np.zeros((NH, NH * 128), np.float32)
    for h in range(NH):
        rs[h, h * 128:(h + 1) * 128] = 1.0
    rs_d = P.const_in("c_rowsel", rs)
    tri01 = np.triu(np.ones((128, 128), dtype=np.float32))
    tri_d = P.const_in("c_tri01", tri01)

    P.ident_sb = P.sbuf("ident", [128, 128], BF16)
    identf_sb = P.sbuf("identf", [128, 128], F32)
    maskl_sb = P.sbuf("maskl", [128, 128], F32)
    onesf_sb = P.sbuf("onesf", [1, 128], F32)
    ucs_sb = P.sbuf("ucs", [128, 128], F32)
    nucs_sb = P.sbuf("nucs", [128, 128], F32)
    rs_sb = P.sbuf("rowsel", [NH, NH * 128], F32)
    tri_sb = P.sbuf("tri01", [128, 128], F32)

    x_tm = P.sbuf("x_tm", [128, NT * DI], BF16)
    b_fm = P.sbuf("b_fm", [128, T], BF16)
    c_fm = P.sbuf("c_fm", [128, T], BF16)
    b_tm = P.sbuf("b_tm", [128, NT * 128], BF16)
    y_tm = P.sbuf("y_tm", [128, NT * DI], BF16)
    S_all = P.sbuf("S_all", [128, NH * HD], F32)
    dt_tm = P.sbuf("dt_tm", [128, NT * NH], F32)
    acs_tm = P.sbuf("acs_tm", [128, NT * NH], F32)
    nacs_tm = P.sbuf("nacs_tm", [128, NT * NH], F32)
    e_tm = P.sbuf("e_tm", [128, NT * NH], F32)
    dd_tm = P.sbuf("dd_tm", [128, NT * NH], F32)
    gam_b = P.sbuf("gam_b", [128, NT * NH], F32)
    alast_b = None
    acs_fm = P.sbuf("acs_fm", [NH, T], F32)
    arow = None
    dtb_b = P.sbuf("dtb_b", [128, NH], F32)
    nega_b = P.sbuf("nega_b", [128, NH], F32)
    dexp = P.sbuf("dexp", [128, DI], BF16)
    nw_b = P.sbuf("nw_b", [128, DI], BF16)
    t1 = P.sbuf("t1", [128, NT * NH], F32)
    alast_b = t1
    rowbuf = P.sbuf("rowbuf", [1, NH], F32)
    cbm = P.sbuf("cbm", [128, 128], BF16)
    w_sb = P.sbuf("w_sbm", [128, 128], BF16)
    bsc = P.sbuf("bsc", [128, 128], BF16)
    cse = P.sbuf("cse", [128, NH * HD // 4], F32)
    u_sb = t1
    chrot = P.sbuf("chrot", [128, T], BF16)
    ow_sb = y_tm
    zrot = None
    sumsq = P.sbuf("sumsq", [128, NT], F32)
    rstd = P.sbuf("rstd", [128, NT], F32)
    xadd = None

    ps_y = P.psum("ps_y", [128, 512], F32)
    ps_cs = P.psum("ps_cs", [128, 512], F32)
    ps_stt = P.psum("ps_stt", [128, 512], F32)
    ps_arg = P.psum("ps_arg", [128, 512], F32)
    ps_tr = P.psum("ps_tr", [128, 1024], BF16)
    ps_m = P.psum("ps_m", [128, 512], F32)

    s_c = P.sem("s_c")
    s_cg = P.sem("s_cg")
    s_r = P.sem("s_r")
    s_st = [P.sem("st0"), P.sem("st1")]
    s_xa = [P.sem("s_xa0"), P.sem("s_xa1")]

    ncst = 0
    for dst, src in ((P.ident_sb, ident_d), (identf_sb, identf_d),
                     (maskl_sb, maskl_d), (onesf_sb, onesf_d),
                     (ucs_sb, ucs_d), (nucs_sb, nucs_d), (rs_sb, rs_d),
                     (tri_sb, tri_d)):
        s_c.next(16)
        ncst += 16
        P.on("sync", lambda e, dst=dst, src=src, s=s_c.h: e.dma_start(
            out=dst[:, :], in_=src[:, :]).then_inc(s, 16))
    s_bc = P.sem("s_bc")
    P.on("sync", lambda e, s=s_bc.h: e.dma_start(
        out=b_fm[:, :], in_=xbc_i[:, 16 * T:17 * T]).then_inc(s, 16))
    P.on("sync", lambda e, s=s_bc.h: e.dma_start(
        out=c_fm[:, :], in_=xbc_i[:, 17 * T:18 * T]).then_inc(s, 16))
    P.on("sync", lambda e, s=s_bc.h: e.dma_start(
        out=dt_tm[:, :], in_=dt_i[:, :]).then_inc(s, 16))
    vbc = s_bc.next(48)
    for dst, src in ((dtb_b, dtb_d), (nega_b, alog_d)):
        pass
    s_rw = P.sem("s_rw")
    P.on("sync", lambda e, s=s_rw.h: e.dma_start(
        out=rowbuf[0:1, :], in_=dtb_d).then_inc(s, 16))
    vrw1 = s_rw.next(16)
    s_ow = P.sem("s_ow")
    vow = None

    s_ld = P.sem("s_ld")
    s_xt = P.sem("s_xt")
    s_xte = P.sem("s_xte")
    P.wait("tensor", s_c, ncst)
    for ch in range(16):
        vld = s_ld.next(16)
        if ch >= 1:
            P.wait("sync", s_xt, 2 * ch)
        P.on("sync", lambda e, ch=ch, s=s_ld.h: e.dma_start(
            out=chrot[:, :], in_=xbc_i[:, ch * T:(ch + 1) * T]
        ).then_inc(s, 16))
        P.wait("tensor", s_ld, vld)
        for g in range(2):
            if 2 * ch + g >= 1:
                P.wait("tensor", s_xte, 2 * ch + g)
            vxt = s_xt.next(1)

            def _xt(e, g=g, s=s_xt.h):
                for i in range(8):
                    ins = e.transpose(
                        ps_tr[:, i * 128:(i + 1) * 128],
                        chrot[:, (g * 8 + i) * 128:(g * 8 + i) * 128 + 128],
                        P.ident_sb[:, :])
                ins.then_inc(s, 1)
            P.on("tensor", _xt)
            P.wait("vector", s_xt, vxt)
            s_xte.next(1)
            P.on("vector", lambda e, ch=ch, g=g, s=s_xte.h: e.tensor_copy(
                bass.AP(x_tm.ap().tensor, g * 8 * DI + ch * 128,
                        [[x_tm.ap().ap[0][0], 128], [DI, 8], [1, 128]]),
                ps_tr[:, 0:1024]).then_inc(s, 1))
    n_xte = s_xte.n
    P.dbg = dbg
    emit_m2_rest(P, locals())
    return P


def emit_m2_rest(P, L):
    g = L
    (x_tm, b_fm, c_fm, b_tm, y_tm, S_all, dt_tm, acs_tm, nacs_tm, e_tm,
     dd_tm, gam_b, alast_b, acs_fm, arow, dtb_b, nega_b, dexp, nw_b, t1,
     rowbuf, cbm, w_sb, bsc, cse, u_sb, chrot, ow_sb, zrot, sumsq, rstd,
     xadd) = (g[k] for k in (
         "x_tm", "b_fm", "c_fm", "b_tm", "y_tm", "S_all", "dt_tm", "acs_tm",
         "nacs_tm", "e_tm", "dd_tm", "gam_b", "alast_b", "acs_fm", "arow",
         "dtb_b", "nega_b", "dexp", "nw_b", "t1", "rowbuf", "cbm", "w_sbm"
         if False else "w_sb", "bsc", "cse", "u_sb", "chrot", "ow_sb",
         "zrot", "sumsq", "rstd", "xadd"))
    (ps_y, ps_cs, ps_stt, ps_arg, ps_tr, ps_m) = (g[k] for k in (
        "ps_y", "ps_cs", "ps_stt", "ps_arg", "ps_tr", "ps_m"))
    (identf_sb, maskl_sb, onesf_sb, ucs_sb, nucs_sb, rs_sb, tri_sb) = (
        g[k] for k in ("identf_sb", "maskl_sb", "onesf_sb", "ucs_sb",
                       "nucs_sb", "rs_sb", "tri_sb"))
    (xbc_i, z_i, dt_i, dtb_d, alog_d, dpar_d, nw_d, x_in, x_out) = (
        g[k] for k in ("xbc_i", "z_i", "dt_i", "dtb_d", "alog_d", "dpar_d",
                       "nw_d", "x_in", "x_out"))
    ow_sb = g["y_tm"]
    s_bc, s_r, s_rw, s_ow = (g[k] for k in ("s_bc", "s_r", "s_rw", "s_ow"))
    vbc, vrw1 = g["vbc"], g["vrw1"]
    s_st, s_xa = g["s_st"], g["s_xa"]
    s_xte, n_xte = g["s_xte"], g["n_xte"]

    s_bt = P.sem("s_btm")
    s_bte = P.sem("s_btme")
    P.wait("tensor", s_bc, vbc)
    P.wait("tensor", s_xte, n_xte)
    for gg in range(2):
        if gg >= 1:
            P.wait("tensor", s_bte, gg)
        vbt = s_bt.next(1)

        def _bt(e, gg=gg, s=s_bt.h):
            for i in range(8):
                ins = e.transpose(ps_tr[:, i * 128:(i + 1) * 128],
                                  b_fm[:, (gg * 8 + i) * 128:
                                       (gg * 8 + i) * 128 + 128],
                                  P.ident_sb[:, :])
            ins.then_inc(s, 1)
        P.on("tensor", _bt)
        P.wait("vector", s_bt, vbt)
        s_bte.next(1)
        P.on("vector", lambda e, gg=gg, s=s_bte.h: e.tensor_copy(
            ap(b_tm, gg * 8 * 128, [[1, 1024]]),
            ps_tr[:, 0:1024]).then_inc(s, 1))

    s_dm = P.sem("s_dm")
    s_ds = P.sem("s_ds")
    P.wait("tensor", s_rw, vrw1)
    vq = s_dm.next(1)
    P.on("tensor", lambda e, s=s_dm.h: e.matmul(
        ps_m[:, 0:NH], onesf_sb[0:1, :], rowbuf[0:1, :],
        start=True, stop=True).then_inc(s, 1))
    P.wait("vector", s_dm, vq)
    vq2 = s_dm.next(1)
    P.on("vector", lambda e, s=s_dm.h: e.tensor_copy(
        dtb_b[:, :], ps_m[:, 0:NH]).then_inc(s, 1))
    P.wait("sync", s_dm, vq)
    vrw2 = s_rw.next(16)
    P.on("sync", lambda e, s=s_rw.h: e.dma_start(
        out=rowbuf[0:1, :], in_=alog_d).then_inc(s, 16))
    P.wait("scalar", s_rw, vrw2)
    vds = s_ds.next(1)
    P.on("scalar", lambda e, s=s_ds.h: e.activation(
        rowbuf[0:1, :], rowbuf[0:1, :], AF.Exp).then_inc(s, 1))
    P.wait("tensor", s_ds, vds)
    vq3 = s_dm.next(1)
    P.on("tensor", lambda e, s=s_dm.h: e.matmul(
        ps_m[:, 64:64 + NH], onesf_sb[0:1, :], rowbuf[0:1, :],
        start=True, stop=True).then_inc(s, 1))
    P.wait("vector", s_dm, vq3)
    vq4 = s_dm.next(1)
    P.on("vector", lambda e, s=s_dm.h: e.tensor_scalar(
        nega_b[:, :], ps_m[:, 64:64 + NH], -1.0, None,
        ALU.mult).then_inc(s, 1))
    P.wait("sync", s_dm, vq3)
    vrw3 = s_rw.next(16)
    P.on("sync", lambda e, s=s_rw.h: e.dma_start(
        out=rowbuf[0:1, :], in_=dpar_d).then_inc(s, 16))
    P.wait("tensor", s_rw, vrw3)
    vq5 = s_dm.next(1)
    P.on("tensor", lambda e, s=s_dm.h: e.matmul(
        ps_m[:, 128:128 + NH], onesf_sb[0:1, :], rowbuf[0:1, :],
        start=True, stop=True).then_inc(s, 1))
    P.wait("vector", s_dm, vq5)
    vq6 = s_dm.next(1)
    P.on("vector", lambda e, s=s_dm.h: e.tensor_copy(
        dexp[:, :],
        bass.AP(ps_m.ap().tensor, 128, [[ps_m.ap().ap[0][0], 128],
                                        [1, NH], [0, HD]])).then_inc(s, 1))
    nwrow = P.sbuf("nwrow", [1, 512], F32)
    arow = nwrow
    s_nw = P.sem("s_nw")
    for nc2 in range(4):
        vnw = s_nw.next(16)
        P.on("sync", lambda e, nc2=nc2, s=s_nw.h: e.dma_start(
            out=nwrow[0:1, :],
            in_=nw_d[nc2 * 512:(nc2 + 1) * 512]).then_inc(s, 16))
        P.wait("tensor", s_nw, vnw)
        vqa = s_dm.next(1)
        P.on("tensor", lambda e, nc2=nc2, s=s_dm.h: e.matmul(
            ps_arg[:, 0:512], onesf_sb[0:1, :], nwrow[0:1, :],
            start=True, stop=True).then_inc(s, 1))
        P.wait("vector", s_dm, vqa)
        vqb = s_dm.next(1)
        P.on("vector", lambda e, nc2=nc2, s=s_dm.h: e.tensor_copy(
            nw_b[:, nc2 * 512:(nc2 + 1) * 512],
            ps_arg[:, 0:512]).then_inc(s, 1))
        P.wait("tensor", s_dm, vqb)
        P.wait("sync", s_dm, vqa)
    s_si = P.sem("s_si")
    vsi = s_si.next(1)
    P.on("vector", lambda e, s=s_si.h: e.memset(
        S_all[:, :], 0.0).then_inc(s, 1))

    s_dt = P.sem("s_dt")
    P.wait("vector", s_bc, vbc)
    P.wait("vector", s_dm, vq2)
    d0 = s_dt.n
    s_dt.next(6)
    P.on("vector", lambda e, s=s_dt.h: e.tensor_tensor(
        dt_tm[:, :], dt_tm[:, :],
        bass.AP(dtb_b.ap().tensor, 0, [dtb_b.ap().ap[0], [0, NT], [1, NH]]),
        ALU.add).then_inc(s, 1))
    P.wait("scalar", s_dt, d0 + 1)
    P.on("scalar", lambda e, s=s_dt.h: e.activation(
        t1[:, :], dt_tm[:, :], AF.Abs).then_inc(s, 1))
    P.wait("scalar", s_dt, d0 + 2)
    P.on("scalar", lambda e, s=s_dt.h: e.activation(
        t1[:, :], t1[:, :], AF.Exp, scale=-1.0).then_inc(s, 1))
    P.wait("vector", s_dt, d0 + 3)
    P.on("vector", lambda e, s=s_dt.h: e.tensor_scalar(
        t1[:, :], t1[:, :], 1.0, None, ALU.add).then_inc(s, 1))
    P.wait("scalar", s_dt, d0 + 4)
    P.on("scalar", lambda e, s=s_dt.h: e.activation(
        t1[:, :], t1[:, :], AF.Ln).then_inc(s, 1))
    P.wait("vector", s_dt, d0 + 5)
    P.on("vector", lambda e, s=s_dt.h: e.scalar_tensor_tensor(
        dt_tm[:, :], dt_tm[:, :], 0.0, t1[:, :], ALU.max,
        ALU.add).then_inc(s, 1))
    P.wait("vector", s_dt, d0 + 6)
    P.wait("vector", s_dm, vq4)
    vdt7 = s_dt.next(1)
    P.on("vector", lambda e, s=s_dt.h: e.tensor_tensor(
        t1[:, :], dt_tm[:, :],
        bass.AP(nega_b.ap().tensor, 0, [nega_b.ap().ap[0], [0, NT],
                                        [1, NH]]),
        ALU.mult).then_inc(s, 1))
    s_cs2 = P.sem("s_cs2")
    s_cse = P.sem("s_cse2")
    P.wait("tensor", s_dt, vdt7)
    for i in range(NT):
        vcs = s_cs2.next(1)
        if s_cse.n >= 1:
            P.wait("tensor", s_cse, s_cse.n)

        def _cs(e, i=i, s=s_cs2.h):
            e.matmul(ps_arg[:, 0:NH], ucs_sb[:, :],
                     t1[:, i * NH:(i + 1) * NH], start=True, stop=True)
            ins = e.matmul(ps_arg[:, 64:64 + NH], nucs_sb[:, :],
                           t1[:, i * NH:(i + 1) * NH],
                           start=True, stop=True)
            ins.then_inc(s, 1)
        P.on("tensor", _cs)
        P.wait("vector", s_cs2, vcs)
        s_cse.next(2)
        P.on("vector", lambda e, i=i, s=s_cse.h: e.tensor_copy(
            acs_tm[:, i * NH:(i + 1) * NH], ps_arg[:, 0:NH]).then_inc(s, 1))
        P.on("vector", lambda e, i=i, s=s_cse.h: e.tensor_copy(
            nacs_tm[:, i * NH:(i + 1) * NH],
            ps_arg[:, 64:64 + NH]).then_inc(s, 1))
    vcse = s_cse.n
    s_e = P.sem("s_e")
    P.wait("scalar", s_cse, vcse)
    ve1 = s_e.next(1)
    P.on("scalar", lambda e, s=s_e.h: e.activation(
        e_tm[:, :], acs_tm[:, :], AF.Exp).then_inc(s, 1))
    P.wait("sync", s_cse, vcse)
    P.wait("sync", s_dm, s_dm.n)
    vr = s_r.next(16)
    P.on("sync", lambda e, s=s_r.h: e.dma_start(
        out=arow[0:1, 0:NT * NH],
        in_=acs_tm[127:128, :]).then_inc(s, 16))
    P.wait("tensor", s_r, vr)
    ve2 = s_e.next(1)
    P.on("tensor", lambda e, s=s_e.h: e.matmul(
        ps_arg[:, 0:512], onesf_sb[0:1, :], arow[0:1, 0:512],
        start=True, stop=True).then_inc(s, 1))
    P.wait("vector", s_e, ve2)
    ve3 = s_e.next(1)
    P.on("vector", lambda e, s=s_e.h: e.tensor_copy(
        alast_b[:, :], ps_arg[:, 0:512]).then_inc(s, 1))
    P.wait("scalar", s_e, ve3)
    ve4 = s_e.next(1)
    P.on("scalar", lambda e, s=s_e.h: e.activation(
        gam_b[:, :], alast_b[:, :], AF.Exp).then_inc(s, 1))
    P.wait("vector", s_e, ve4)
    ve5 = s_e.next(1)
    P.on("vector", lambda e, s=s_e.h: e.tensor_sub(
        alast_b[:, :], alast_b[:, :], acs_tm[:, :]).then_inc(s, 1))
    P.wait("scalar", s_e, ve5)
    ve6 = s_e.next(1)
    P.on("scalar", lambda e, s=s_e.h: e.activation(
        alast_b[:, :], alast_b[:, :], AF.Exp).then_inc(s, 1))
    P.wait("vector", s_e, ve6)
    ve7 = s_e.next(1)
    P.on("vector", lambda e, s=s_e.h: e.tensor_mul(
        dd_tm[:, :], alast_b[:, :], dt_tm[:, :]).then_inc(s, 1))
    s_at = P.sem("s_at")
    s_ate = P.sem("s_ate")
    P.wait("tensor", s_cse, vcse)
    P.wait("tensor", s_e, ve3)
    for i in range(NT):
        vat = s_at.next(1)
        if s_ate.n >= 1:
            P.wait("tensor", s_ate, s_ate.n)
        P.on("tensor", lambda e, i=i, s=s_at.h: e.transpose(
            bass.AP(ps_arg.ap().tensor, 0,
                    [[ps_arg.ap().ap[0][0], NH], [1, 128]]),
            bass.AP(acs_tm.ap().tensor, i * NH,
                    [[acs_tm.ap().ap[0][0], 128], [1, NH]]),
            identf_sb[:, :]).then_inc(s, 1))
        P.wait("vector", s_at, vat)
        s_ate.next(1)
        P.on("vector", lambda e, i=i, s=s_ate.h: e.tensor_copy(
            bass.AP(acs_fm.ap().tensor, i * 128,
                    [[acs_fm.ap().ap[0][0], NH], [1, 128]]),
            bass.AP(ps_arg.ap().tensor, 0,
                    [[ps_arg.ap().ap[0][0], NH], [1, 128]])
        ).then_inc(s, 1))
    vate = s_ate.n

    s_cb = P.sem("s_cb")
    s_cbe = P.sem("s_cbe")
    s_ar = P.sem("s_ar")
    s_we = P.sem("s_we")
    s_ws = P.sem("s_ws")
    s_ym = P.sem("s_ym")
    s_cm = P.sem("s_cm")
    s_bs = P.sem("s_bs")
    s_sm = P.sem("s_sm")
    s_ev = P.sem("s_ev")
    s_su = P.sem("s_su")
    P.wait("vector", s_e, ve7)
    P.wait("vector", s_si, vsi)
    P.wait("tensor", s_ate, vate)
    P.wait("tensor", s_bte, 2)
    P.wait("vector", s_xte, n_xte)
    P.wait("tensor", s_e, ve2 if False else s_e.n)
    s_sbf = P.sem("s_sbf")
    for c in range(NT):
        if c > 0:
            P.wait("vector", s_su, s_su.n)
            vsbf = s_sbf.next(1)
            P.on("vector", lambda e, s=s_sbf.h: e.tensor_copy(
                chrot[:, :], S_all[:, :]).then_inc(s, 1))
            P.wait("tensor", s_sbf, vsbf)
        vcb = s_cb.next(1)
        if s_cbe.n >= 1:
            P.wait("tensor", s_cbe, s_cbe.n)
        P.on("tensor", lambda e, c=c, s=s_cb.h: e.matmul(
            ps_m[:, 0:128], b_fm[:, c * 128:(c + 1) * 128],
            c_fm[:, c * 128:(c + 1) * 128], start=True,
            stop=True).then_inc(s, 1))
        P.wait("vector", s_cb, vcb)
        vcbe = s_cbe.next(1)
        P.on("vector", lambda e, s=s_cbe.h: e.tensor_mul(
            cbm[:, :], ps_m[:, 0:128], tri_sb[:, :]).then_inc(s, 1))
        P.wait("scalar", s_cbe, vcbe)
        P.wait("tensor", s_cbe, vcbe)
        for hg in range(4):
            for hh in range(8):
                h = hg * 8 + hh
                var = s_ar.next(1)
                if s_we.n >= 1:
                    P.wait("tensor", s_we, s_we.n)

                def _ar(e, c=c, h=h, s=s_ar.h):
                    e.matmul(ps_arg[:, 0:128],
                             rs_sb[:, h * 128:(h + 1) * 128],
                             acs_fm[:, c * 128:(c + 1) * 128],
                             start=True, stop=False)
                    ins = e.matmul(ps_arg[:, 0:128], identf_sb[:, :],
                                   maskl_sb[:, :], start=False, stop=True)
                    ins.then_inc(s, 1)
                P.on("tensor", _ar)
                P.wait("scalar", s_ar, var)
                if s_ws.n >= 1:
                    P.wait("scalar", s_ws, s_ws.n)
                vwe = s_we.next(1)
                P.on("scalar", lambda e, c=c, h=h, s=s_we.h: e.activation(
                    w_sb[:, :], ps_arg[:, 0:128], AF.Exp,
                    bias=nacs_tm[:, c * NH + h:c * NH + h + 1]
                ).then_inc(s, 1))
                P.wait("vector", s_we, vwe)
                vws = s_ws.next(1)
                P.on("vector", lambda e, c=c, h=h, s=s_ws.h:
                     e.scalar_tensor_tensor(
                         w_sb[:, :], w_sb[:, :],
                         dt_tm[:, c * NH + h:c * NH + h + 1], cbm[:, :],
                         ALU.mult, ALU.mult).then_inc(s, 1))
                P.wait("tensor", s_ws, vws)
                if hh == 0 and s_ev.n >= 1:
                    P.wait("tensor", s_ev, s_ev.n)
                vym = s_ym.next(1)
                P.on("tensor", lambda e, c=c, h=h, hh=hh,
                     s=s_ym.h: e.matmul(
                         ps_y[:, hh * 64:(hh + 1) * 64], w_sb[:, :],
                         x_tm[:, c * DI + h * 64:c * DI + h * 64 + 64],
                         start=True, stop=True).then_inc(s, 1))
                if c > 0:
                    vcm = s_cm.next(1)
                    P.on("tensor", lambda e, c=c, h=h, hh=hh,
                         s=s_cm.h: e.matmul(
                             ps_cs[:, hh * 64:(hh + 1) * 64],
                             c_fm[:, c * 128:(c + 1) * 128],
                             chrot[:, h * 64:(h + 1) * 64],
                             start=True, stop=True).then_inc(s, 1))
                if c < NT - 1:
                    vbs = s_bs.next(1)
                    if s_sm.n >= 1:
                        P.wait("vector", s_sm, s_sm.n)
                    P.on("vector", lambda e, c=c, h=h, s=s_bs.h:
                         e.tensor_scalar(
                             bsc[:, :], ap(b_tm, c * 128, [[1, 128]]),
                             dd_tm[:, c * NH + h:c * NH + h + 1], None,
                             ALU.mult).then_inc(s, 1))
                    P.wait("tensor", s_bs, vbs)
                    if hh == 0 and s_su.n >= 1:
                        P.wait("tensor", s_su, s_su.n)
                    vsm = s_sm.next(1)
                    P.on("tensor", lambda e, c=c, h=h, hh=hh,
                         s=s_sm.h: e.matmul(
                             ps_stt[:, hh * 64:(hh + 1) * 64], bsc[:, :],
                             x_tm[:, c * DI + h * 64:c * DI + h * 64 + 64],
                             start=True, stop=True).then_inc(s, 1))
            P.wait("vector", s_ym, s_ym.n)
            if c > 0:
                P.wait("vector", s_cm, s_cm.n)
            e0 = s_ev.n
            nev = 4 if c > 0 else 3
            s_ev.next(nev)
            h0 = hg * 8
            if c > 0:
                P.on("vector", lambda e, c=c, h0=h0, s=s_ev.h: e.tensor_mul(
                    cse[:, :], ps_cs[:, :],
                    bass.AP(e_tm.ap().tensor, c * NH + h0,
                            [e_tm.ap().ap[0], [1, 8], [0, HD]])
                ).then_inc(s, 1))
                P.wait("vector", s_ev, e0 + 1)
                P.on("vector", lambda e, s=s_ev.h: e.tensor_add(
                    u_sb[:, :], ps_y[:, :], cse[:, :]).then_inc(s, 1))
                P.wait("vector", s_ev, e0 + 2)
            else:
                P.on("vector", lambda e, s=s_ev.h: e.tensor_copy(
                    u_sb[:, :], ps_y[:, :]).then_inc(s, 1))
                P.wait("vector", s_ev, e0 + 1)
            P.on("vector", lambda e, c=c, h0=h0, s=s_ev.h: e.tensor_mul(
                cse[:, :], x_tm[:, c * DI + h0 * 64:c * DI + h0 * 64 + 512],
                dexp[:, h0 * 64:h0 * 64 + 512]).then_inc(s, 1))
            P.wait("vector", s_ev, e0 + nev - 1)
            P.on("vector", lambda e, c=c, h0=h0, s=s_ev.h: e.tensor_add(
                y_tm[:, c * DI + h0 * 64:c * DI + h0 * 64 + 512],
                u_sb[:, :], cse[:, :]).then_inc(s, 1))
            if c < NT - 1:
                P.wait("vector", s_sm, s_sm.n)
                u0 = s_su.n
                s_su.next(2)
                P.on("vector", lambda e, c=c, h0=h0, s=s_su.h: e.tensor_mul(
                    S_all[:, h0 * 64:h0 * 64 + 512],
                    S_all[:, h0 * 64:h0 * 64 + 512],
                    bass.AP(gam_b.ap().tensor, c * NH + h0,
                            [gam_b.ap().ap[0], [1, 8], [0, HD]])
                ).then_inc(s, 1))
                P.wait("vector", s_su, u0 + 1)
                P.on("vector", lambda e, h0=h0, s=s_su.h: e.tensor_add(
                    S_all[:, h0 * 64:h0 * 64 + 512],
                    S_all[:, h0 * 64:h0 * 64 + 512],
                    ps_stt[:, :]).then_inc(s, 1))

    if getattr(P, "dbg", False):
        dbg_y = P.dram_out("dbg_y", [128, NT * DI])
        dbg_meta = P.dram_out("dbg_meta", [128, 5 * NT * NH])
        s_dbg = P.sem("s_dbg")
        P.wait("gpsimd", s_ev, s_ev.n)
        P.on("gpsimd", lambda e, s=s_dbg.h: e.dma_start(
            out=dbg_y[:, :], in_=y_tm[:, :]).then_inc(s, 16))
        P.wait("gpsimd", s_dt, s_dt.n)
        P.wait("gpsimd", s_e, s_e.n)
        for j, tns in enumerate((dt_tm, acs_tm, e_tm, dd_tm, gam_b)):
            P.on("gpsimd", lambda e, j=j, tns=tns, s=s_dbg.h: e.dma_start(
                out=dbg_meta[:, j * NT * NH:(j + 1) * NT * NH],
                in_=tns[:, :]).then_inc(s, 16))
        P.wait("gpsimd", s_dbg, 96)

    zrot = [b_fm, b_fm]
    s_z = P.sem("s_z")
    s_gs = P.sem("s_gs")
    s_gm = P.sem("s_gm")
    s_qs = P.sem("s_qs")
    P.wait("vector", s_ev, s_ev.n)
    P.wait("sync", s_cb, 16)
    P.wait("scalar", s_cm, s_cm.n)
    for i in range(NT):
        slot = i % 2
        if i >= 1:
            P.wait("sync", s_gs, i)
        vz = s_z.next(16)
        P.on("sync", lambda e, i=i, slot=slot, s=s_z.h: e.dma_start(
            out=zrot[slot][:, :],
            in_=z_i[:, i * DI:(i + 1) * DI]).then_inc(s, 16))
        P.wait("scalar", s_z, vz)
        if i >= 1:
            P.wait("scalar", s_gm, i)
        vgs = s_gs.next(1)
        P.on("scalar", lambda e, slot=slot, s=s_gs.h: e.activation(
            chrot[:, :], zrot[slot][:, :], AF.Silu).then_inc(s, 1))
        P.wait("vector", s_gs, vgs)
        vgm = s_gm.next(1)
        P.on("vector", lambda e, i=i, s=s_gm.h: e.tensor_mul(
            ap(y_tm, i * DI, [[1, DI]]), ap(y_tm, i * DI, [[1, DI]]),
            chrot[:, :]).then_inc(s, 1))
        P.wait("scalar", s_gm, vgm)
        s_qs.next(1)
        P.on("scalar", lambda e, i=i, s=s_qs.h: e.activation(
            chrot[:, :], ap(y_tm, i * DI, [[1, DI]]), AF.Square,
            accum_out=sumsq[:, i:i + 1]).then_inc(s, 1))
    s_rs = P.sem("s_rs")
    P.wait("vector", s_qs, s_qs.n)
    r0 = s_rs.n
    s_rs.next(2)
    P.on("vector", lambda e, s=s_rs.h: e.tensor_scalar(
        rstd[:, :], sumsq[:, :], 1.0 / DI, 1e-5, ALU.mult,
        ALU.add).then_inc(s, 1))
    P.wait("scalar", s_rs, r0 + 1)
    P.on("scalar", lambda e, s=s_rs.h: e.activation(
        rstd[:, :], rstd[:, :], AF.Sqrt).then_inc(s, 1))
    P.wait("vector", s_rs, r0 + 2)
    vr3 = s_rs.next(1)
    P.on("vector", lambda e, s=s_rs.h: e.reciprocal(
        rstd[:, :], rstd[:, :]).then_inc(s, 1))
    P.wait("vector", s_rs, vr3)
    s_fs = P.sem("s_fs")
    for i in range(NT):
        s_fs.next(1)
        P.on("vector", lambda e, i=i, s=s_fs.h: e.scalar_tensor_tensor(
            ap(y_tm, i * DI, [[1, DI]]), ap(y_tm, i * DI, [[1, DI]]),
            rstd[:, i:i + 1], nw_b[:, :], ALU.mult,
            ALU.mult).then_inc(s, 1))
    s_yt = P.sem("s_yt")
    s_yte = P.sem("s_yte")
    P.wait("tensor", s_fs, s_fs.n)
    for i in range(NT):
        for gg in range(2):
            if 2 * i + gg >= 1:
                P.wait("tensor", s_yte, 2 * i + gg)
            vyt = s_yt.next(1)

            def _yt(e, i=i, gg=gg, s=s_yt.h):
                for k in range(8):
                    ins = e.transpose(
                        ps_tr[:, k * 128:(k + 1) * 128],
                        ap(y_tm, i * DI + (gg * 8 + k) * 128, [[1, 128]]),
                        P.ident_sb[:, :])
                ins.then_inc(s, 1)
            P.on("tensor", _yt)
            P.wait("vector", s_yt, vyt)
            s_yte.next(1)
            P.on("vector", lambda e, i=i, gg=gg, s=s_yte.h: e.tensor_copy(
                bass.AP(x_tm.ap().tensor, gg * 8 * T + i * 128,
                        [[x_tm.ap().ap[0][0], 128], [T, 8], [1, 128]]),
                ps_tr[:, 0:1024]).then_inc(s, 1))
    vyte = s_yte.n

    xadd = [bass.AP(S_all.ap().tensor, j * D,
                    [S_all.ap().ap[0], [1, D]]) for j in range(2)]
    ow_d = g["ow_d"]
    s_of = P.sem("s_of")
    s_ou = P.sem("s_ou")
    P.wait("gpsimd", s_yte, vyte)
    P.wait("sync", P.sem("s_sbf"), 15)
    vow2 = s_ow.next(16)
    P.on("gpsimd", lambda e, s=s_ow.h: e.dma_start(
        out=ap(ow_sb, 0, [[D, NT], [1, D]]),
        in_=ow_d.rearrange("(c p) n -> p c n", p=128)).then_inc(s, 16))
    P.wait("tensor", s_yte, vyte)
    P.wait("tensor", s_ow, vow2)
    for i in range(NT):
        par = i % 2
        if i >= 2:
            P.wait("sync", s_st[par], 16 * (i // 2))
        vxa = s_xa[par].next(16)
        P.on("sync", lambda e, i=i, par=par, s=s_xa[par].h: e.dma_start(
            out=xadd[par],
            in_=x_in.rearrange("(n p) d -> n p d", p=128)[i]
        ).then_inc(s, 16))
        for nc2 in range(2):
            psu = ps_y if nc2 == 0 else ps_cs
            vof = s_of.next(1)
            if s_ou.n >= 2:
                P.wait("tensor", s_ou, s_ou.n - 1)

            def _of(e, i=i, nc2=nc2, psu=psu, s=s_of.h):
                for kc in range(NT):
                    ins = e.matmul(
                        psu[:, 0:512],
                        bass.AP(x_tm.ap().tensor, kc * T + i * 128,
                                [[x_tm.ap().ap[0][0], 128], [1, 128]]),
                        ow_sb[:, kc * D + nc2 * 512:
                              kc * D + nc2 * 512 + 512],
                        start=(kc == 0), stop=(kc == NT - 1))
                ins.then_inc(s, 1)
            P.on("tensor", _of)
            P.wait("vector", s_of, vof)
            if nc2 == 0:
                P.wait("vector", s_xa[par], vxa)
            s_ou.next(1)
            sub = bass.AP(S_all.ap().tensor, par * D + nc2 * 512,
                          [S_all.ap().ap[0], [1, 512]])
            P.on("vector", lambda e, sub=sub, psu=psu,
                 s=s_ou.h: e.tensor_add(
                     sub, sub, psu[:, 0:512]).then_inc(s, 1))
        P.wait("sync", s_ou, s_ou.n)
        s_st[par].next(16)
        P.on("sync", lambda e, i=i, par=par, s=s_st[par].h: e.dma_start(
            out=x_out.rearrange("(n p) d -> n p d", p=128)[i],
            in_=xadd[par]).then_inc(s, 16))
    P.wait("sync", s_st[0], s_st[0].n)
    P.wait("sync", s_st[1], s_st[1].n)
